# version 14 (final)
# baseline (speedup 1.0000x reference)
import math
import numpy as np
import ml_dtypes
import contextlib
import concourse.bass as bass
import concourse.mybir as mybir
from concourse.bass_utils import run_bass_kernel_spmd

F32 = mybir.dt.float32
BF16 = mybir.dt.bfloat16
AF = mybir.ActivationFunctionType
ALU = mybir.AluOpType
AX = mybir.AxisListType
BFNP = ml_dtypes.bfloat16


class G:
    def __init__(self, S, D, I0, I1, W=512, HALO=512, B=2):
        self.B, self.S, self.D = B, S, D
        self.I0, self.I1, self.W, self.HALO = I0, I1, W, HALO
        self.DC = D // 128
        self.H = self.DC
        self.CH = S // 4
        self.KBUF = S + HALO
        self.KT = self.KBUF // 128
        self.QTOK = self.CH + HALO
        self.QT = self.QTOK // 128
        self.IC0 = I0 // 128
        self.IC1 = I1 // 128
        self.GB1 = self.IC1 // 4
        self.PREW = 3 * self.CH - HALO
        self.SC = 1.0 / math.sqrt(128.0)
        assert D % 128 == 0 and self.CH % 512 == 0 and HALO % 128 == 0
        assert self.KBUF % 128 == 0 and I0 % 128 == 0 and I1 % 512 == 0
        assert self.PREW % 512 == 0 and self.QTOK % 512 == 0
        assert self.W % 128 == 0 and self.HALO >= self.W - 1

    def ttiles(self, total, w=512):
        out, t = [], 0
        while t < total:
            tw = min(w, total - t)
            out.append((t, tw))
            t += tw
        return out


G_FULL = G(S=8192, D=2048, I0=5632, I1=8192)


def stage_host(g, ins):
    f32 = np.float32
    xT = np.ascontiguousarray(np.transpose(np.asarray(ins["L0_attn_x"]), (0, 2, 1))).astype(f32, copy=False)
    sh = {}
    sh["wq"] = np.ascontiguousarray(np.asarray(ins["L0_attn_Wq"], dtype=f32))
    sh["wk"] = np.ascontiguousarray(np.asarray(ins["L0_attn_Wk"], dtype=f32))
    sh["wv"] = np.ascontiguousarray(np.asarray(ins["L0_attn_Wv"], dtype=f32))
    sh["wo"] = np.ascontiguousarray(np.asarray(ins["L0_attn_Wo"], dtype=f32))
    sh["g0T"] = np.ascontiguousarray(np.asarray(ins["L0_ffn_gate_w"], dtype=f32).T)
    sh["u0T"] = np.ascontiguousarray(np.asarray(ins["L0_ffn_up_w"], dtype=f32).T)
    sh["d0T"] = np.ascontiguousarray(np.asarray(ins["L0_ffn_down_w"], dtype=f32).T)
    sh["gs"] = np.ascontiguousarray(np.asarray(ins["L0_ffn_gate_s"], dtype=f32).reshape(g.IC0, 128).T)
    sh["us"] = np.ascontiguousarray(np.asarray(ins["L0_ffn_up_s"], dtype=f32).reshape(g.IC0, 128).T)
    sh["ds"] = np.ascontiguousarray(np.asarray(ins["L0_ffn_down_s"], dtype=f32).reshape(g.DC, 128).T)

    wqkvT = np.asarray(ins["L1_attn_w_qkv"], dtype=f32).T
    cols = []
    for base in (0, g.D):
        for p in range(g.H // 2):
            h0, h1 = 2 * p, 2 * p + 1
            cols.extend(range(base + 128 * h0, base + 128 * h0 + 64))
            cols.extend(range(base + 128 * h1, base + 128 * h1 + 64))
            cols.extend(range(base + 128 * h0 + 64, base + 128 * h0 + 128))
            cols.extend(range(base + 128 * h1 + 64, base + 128 * h1 + 128))
    cols.extend(range(2 * g.D, 3 * g.D))
    sh["wqkvT"] = np.ascontiguousarray(wqkvT[:, cols])
    sh["woutT"] = np.ascontiguousarray(np.asarray(ins["L1_attn_w_out"], dtype=f32).T)

    n = np.arange(g.I1)
    c_, p_ = n // 128, n % 128
    perm1 = 4 * ((c_ % g.GB1) * 128 + p_) + c_ // g.GB1
    sh["w1g"] = np.ascontiguousarray(np.asarray(ins["L1_ffn_w_gate"], dtype=f32)[:, perm1])
    sh["w1u"] = np.ascontiguousarray(np.asarray(ins["L1_ffn_w_up"], dtype=f32)[:, perm1])
    sh["w1d"] = np.ascontiguousarray(np.asarray(ins["L1_ffn_w_down"], dtype=f32)[perm1, :])

    sh["identb"] = np.eye(128, dtype=BFNP)
    sh["identf"] = np.eye(128, dtype=np.float32)
    dg = np.zeros((128, 4 * 512), dtype=BFNP)
    kk = np.arange(128)[:, None]
    qq = np.arange(512)[None, :]
    for d in range(4):
        dg[:, d * 512:(d + 1) * 512] = (128 * d + kk > qq).astype(BFNP)
    sh["l0diag"] = dg

    cores = []
    for c in range(8):
        b, j = c // 4, c % 4
        lo = g.CH * j - g.HALO
        xkv = np.zeros((g.D, g.KBUF), dtype=f32)
        s0 = max(lo, 0)
        xkv[:, s0 - lo:g.S - lo] = xT[b][:, s0:g.S]
        xpre = np.zeros((g.D, g.PREW), dtype=f32)
        if lo > 0:
            xpre[:, :lo] = xT[b][:, :lo]
        bias68 = np.zeros((128, g.KT), dtype=f32)
        valid = g.KBUF - g.CH * j
        tidx = np.arange(g.KT)[None, :] * 128 + np.arange(128)[:, None]
        bias68[tidx >= valid] = -30000.0
        nhalo = g.HALO // 128
        l1m = np.zeros((128, (8 + nhalo) * 512), dtype=BFNP)
        p = np.arange(128)[:, None]
        for d in range(8):
            diff = g.HALO + qq - 128 * d - p
            l1m[:, d * 512:(d + 1) * 512] = ((diff >= 0) & (diff < g.W)).astype(BFNP)
        for d in range(nhalo):
            m = np.array(l1m[:, d * 512:(d + 1) * 512])
            if j == 0:
                m[:] = 0
            l1m[:, (8 + d) * 512:(9 + d) * 512] = m
        islast = np.zeros((128, 512), dtype=BFNP)
        if j == 3:
            islast[:, 511] = 1
        pos = lo + np.arange(g.QTOK)[None, :]
        dfreq = (np.arange(128) % 64).astype(np.float64)
        invf = 1.0 / (10000.0 ** (2.0 * dfreq / 128.0))
        ang = pos * invf[:, None]
        cores.append({
            "xkv": xkv, "xpre": xpre, "bias68": bias68, "l1m": l1m,
            "islast": islast,
            "ropec": np.cos(ang).astype(f32), "ropes": np.sin(ang).astype(f32),
        })
    return sh, cores


ENG = {"t": "tensor", "s": "scalar", "v": "vector", "g": "gpsimd", "y": "sync"}


SERIAL = [False]
SERIAL_PHASES = set()
OWN_WAIT = [False]


class SB:
    def __init__(self):
        self.q = {e: [] for e in ENG}
        self.cnt = {k: 0 for k in ("ts", "ss", "vs", "gc")}
        self.sems = None
        self.chans = {}
        self.real_idx = {e: 0 for e in ENG}
        self.marks = {e: [] for e in ENG}

    def __setattr__(self, k, v):
        if k == "phase" and hasattr(self, "marks"):
            for e in ENG:
                self.marks[e].append((self.real_idx[e], v))
        object.__setattr__(self, k, v)

    OWN = {"v": "vs", "s": "ss", "g": "gc"}
    NPOOL = 88

    def chan_sem(self, chan):
        if chan not in self.chans:
            i = len(self.chans)
            assert i < self.NPOOL, "semaphore pool exhausted"
            self.chans[chan] = "c%d" % i
            self.cnt[self.chans[chan]] = 0
        return self.chans[chan]

    def op(self, e, fn, waits=(), inc=None, n=1, strict=False):
        waits = list(waits)
        own = self.OWN.get(e)
        if own is not None:
            if OWN_WAIT[0] or strict:
                waits.append((own, self.cnt[own]))
            if inc is None:
                inc = own
        if SERIAL[0] or getattr(self, "phase", None) in SERIAL_PHASES:
            waits = waits + [(k, v) for k, v in self.cnt.items() if v > 0]
        if fn is not None:
            self.real_idx[e] += 1
        self.q[e].append((tuple(waits), fn, inc, n))
        if inc is not None:
            self.cnt[inc] += n
            return self.cnt[inc]
        return None

    def dma(self, e, out, in_, waits=(), chan="misc"):
        sem = self.chan_sem(chan)
        gate = (sem, self.cnt[sem])
        self.op(e, (lambda out=out, in_=in_: (lambda eng: eng.dma_start(out=out, in_=in_)))(),
                waits=list(waits) + [gate], inc=sem, n=16)
        return (sem, self.cnt[sem])

    def fence(self, engines=None, sems=None):
        snap = [(k, v) for k, v in self.cnt.items() if v > 0 and (sems is None or k in sems)]
        for e in (engines or ENG):
            self.q[e].append((tuple(snap), None, None, 0))

    def c(self, sem):
        return self.cnt[sem]

    def flush(self, block):
        sems = self.sems
        for e, name in ENG.items():
            ops = self.q[e]
            if not ops:
                continue

            def body(eng, ops=ops):
                seen = {}
                for waits, fn, inc, n in ops:
                    for s, v in waits:
                        if v > 0 and seen.get(s, -1) < v:
                            eng.wait_ge(sems[s], v)
                            seen[s] = v
                    if fn is None:
                        continue
                    r = fn(eng)
                    if inc is not None:
                        r.then_inc(sems[inc], n)
            getattr(block, name)(body)
            self.q[e] = []


def mm_acc(sb, ps, pairs, waits=()):
    n = len(pairs)
    for i, (l, r) in enumerate(pairs):
        def f(eng, l=l, r=r, i=i, n=n, ps=ps):
            return eng.matmul(ps, l, r, start=(i == 0), stop=(i == n - 1))
        sb.op("t", f, waits=waits if i == 0 else (), inc="ts" if i == n - 1 else None)
    return sb.c("ts")


def build_program(g, stop_after=None, dump=()):
    nc = bass.Bass("TRN2", target_bir_lowering=False, debug=False)
    E = lambda n, s, d: nc.dram_tensor(n, s, d, kind="ExternalInput").ap()

    xkv = E("xkv", [g.D, g.KBUF], F32)
    xpre = E("xpre", [g.D, g.PREW], F32)
    wq, wk, wv, wo = (E(n, [g.D, g.D], F32) for n in ("wq", "wk", "wv", "wo"))
    g0T = E("g0T", [g.D, g.I0], F32)
    u0T = E("u0T", [g.D, g.I0], F32)
    d0T = E("d0T", [g.I0, g.D], F32)
    gsI = E("gs", [128, g.IC0], F32)
    usI = E("us", [128, g.IC0], F32)
    dsI = E("ds", [128, g.DC], F32)
    wqkvT = E("wqkvT", [g.D, 3 * g.D], F32)
    woutT = E("woutT", [g.D, g.D], F32)
    w1g = E("w1g", [g.D, g.I1], F32)
    w1u = E("w1u", [g.D, g.I1], F32)
    w1d = E("w1d", [g.I1, g.D], F32)
    bias68I = E("bias68", [128, g.KT], F32)
    l0diagI = E("l0diag", [128, 4 * 512], BF16)
    l1mI = E("l1m", [128, (8 + g.HALO // 128) * 512], BF16)
    islastI = E("islast", [128, 512], BF16)
    ropecI = E("ropec", [128, g.QTOK], F32)
    ropesI = E("ropes", [128, g.QTOK], F32)
    identbI = E("identb", [128, 128], BF16)
    identfI = E("identf", [128, 128], F32)

    y = nc.dram_tensor("y", [g.CH, g.D], F32, kind="ExternalOutput").ap()
    dbg = {}
    shapes = {"k0": [g.D, g.KBUF], "v0t": [g.H * g.KBUF, 128], "q0": [g.D, g.QTOK],
              "stream": [g.D, g.QTOK], "attn1": [g.D, g.CH], "x3": [g.D, g.CH],
              "qr": [g.D, g.CH], "kr": [g.D, g.QTOK], "v1t": [g.H * g.QTOK, 128],
              "hdn": [128, g.IC0 * 512], "hdn1": [128, g.IC1 * 512],
              "g0c": [g.D, g.I0], "u0c": [g.D, g.I0],
              "pg0": [128, 512], "sg0": [128, 512], "su0": [128, 512],
              "streamA": [g.D, g.QTOK]}
    for name in dump:
        dbg[name] = nc.dram_tensor("dbg_" + name, shapes[name], BF16, kind="ExternalOutput").ap()

    k0d = nc.dram_tensor("k0d", [g.D, g.KBUF], BF16).ap()
    v0td = nc.dram_tensor("v0td", [g.H * 128, g.KT * 128], BF16).ap()
    q0d = nc.dram_tensor("q0d", [g.D, g.QTOK], BF16).ap()
    qrd = nc.dram_tensor("qrd", [g.D, g.CH], BF16).ap()
    krd = nc.dram_tensor("krd", [g.D, g.QTOK], BF16).ap()
    v1td = nc.dram_tensor("v1td", [g.H * 128, g.QTOK], BF16).ap()
    x3d = nc.dram_tensor("x3d", [g.D, g.CH], BF16).ap()
    cache_src = {"g0T": g0T, "u0T": u0T, "d0T": d0T, "woutT": woutT,
                 "w1g": w1g, "w1u": w1u, "w1d": w1d}
    cache = {n: nc.dram_tensor(n + "_c", list(src.shape), BF16).ap()
             for n, src in cache_src.items()}
    conv_order = ["g0T", "u0T", "d0T", "woutT", "w1g", "w1u", "w1d"]
    conv_jobs = {}
    for nm in conv_order:
        src = cache_src[nm]
        R = src.shape[0]
        step = 512
        conv_jobs[nm] = [(cache[nm][r:r + min(step, R - r), :], src[r:r + min(step, R - r), :])
                         for r in range(0, R, step)]
    conv_done = {}

    sb = SB()

    with contextlib.ExitStack() as top:
        sems = {}
        for s in ("ts", "ss", "vs", "gc"):
            sems[s] = top.enter_context(nc.semaphore(s))
        for i in range(SB.NPOOL):
            sems["c%d" % i] = top.enter_context(nc.semaphore("c%d" % i))
        sb.sems = sems
        blk = top.enter_context(nc.Block())

        def sbuf(name, shape, dt, stack):
            return stack.enter_context(nc.sbuf_tensor(name, shape, dt))

        def psum(name, shape, dt, stack):
            return stack.enter_context(nc.psum_tensor(name, shape, dt))

        identb = sbuf("identb_s", [128, 128], BF16, top)
        identf = sbuf("identf_s", [128, 128], F32, top)
        ones_cb = sbuf("ones_cb", [128, 1], BF16, top)
        ones_cf = sbuf("ones_cf", [128, 1], F32, top)
        ones_rb = sbuf("ones_rb", [1, 128], BF16, top)
        ones_rf = sbuf("ones_rf", [1, 128], F32, top)
        xsum = sbuf("xsum", [128, g.DC], F32, top)
        xsumb = sbuf("xsumb", [128, g.DC], BF16, top)
        vsum = sbuf("vsum", [128, g.DC], F32, top)
        rowsA = sbuf("rowsA", [1, 1024], F32, top)
        rowsB = sbuf("rowsB", [1, 1024], F32, top)
        rowsC = sbuf("rowsC", [1, 1024], F32, top)
        rowsBF = sbuf("rowsBF", [1, 1024], BF16, top)
        rowsBF2 = sbuf("rowsBF2", [1, 1024], BF16, top)
        xr_t = sbuf("xr_t", [128, 1], F32, top)
        NT1 = len(g.ttiles(g.PREW)) + len(g.ttiles(g.KBUF))
        epsA = sbuf("epsA", [1, 1], F32, top)

        sb.dma("y", identb[:], identbI, chan="cst")
        sb.dma("y", identf[:], identfI, chan="cst")
        sb.op("v", lambda v: v.memset(ones_cb[:], 1.0), inc="vs")
        sb.op("v", lambda v: v.memset(ones_cf[:], 1.0), inc="vs")
        sb.op("v", lambda v: v.memset(ones_rb[:], 1.0), inc="vs")
        sb.op("v", lambda v: v.memset(ones_rf[:], 1.0), inc="vs")
        sb.op("v", lambda v: v.memset(xsum[:], 0.0), inc="vs")
        sb.op("v", lambda v: v.memset(epsA[:], 1e-5), inc="vs")

        def wslice3(src, a, bwidth):
            return src.rearrange("(rc p) x -> p rc x", p=128)[:, :, a:a + bwidth]

        def load_wtile(e, dst, src, a, bwidth, waits=(), chan="w"):
            nch = src.shape[0] // 128
            return sb.dma(e, dst[:, :nch * bwidth].rearrange("p (rc x) -> p rc x", rc=nch),
                          wslice3(src, a, bwidth), waits=waits, chan=chan)

        conv_flat = [(nm, j) for nm in conv_order for j in conv_jobs[nm]]
        conv_pos = [0]

        conv_tok = {}

        def emit_convs(k):
            while k > 0 and conv_pos[0] < len(conv_flat):
                nm, (dst, src) = conv_flat[conv_pos[0]]
                ch = "cv%d" % (conv_pos[0] % 4)
                conv_tok[ch] = sb.dma("g", dst, src, chan=ch)
                conv_done[nm] = list(conv_tok.values())
                conv_pos[0] += 1
                k -= 1

        def ensure_conv(nm):
            while any(n2 == nm for n2, _ in conv_flat[conv_pos[0]:]):
                emit_convs(1)
            return conv_done[nm]

        sb.phase = "P1"
        with contextlib.ExitStack() as P1:
            wk_sb = sbuf("wk_sb", [128, g.DC * g.D], BF16, P1)
            wv_sb = sbuf("wv_sb", [128, g.DC * g.D], BF16, P1)
            xs = [sbuf(f"xs{i}", [128, g.DC * 512], BF16, P1) for i in range(2)]
            kst = [sbuf(f"kst{i}", [128, 512], BF16, P1) for i in range(2)]
            vst = [sbuf(f"vst{i}", [128, 512], BF16, P1) for i in range(2)]
            vtt = [sbuf(f"vtt{i}", [128, 512], BF16, P1) for i in range(2)]
            pp = [psum(f"pp{i}", [128, 512], F32, P1) for i in range(2)]
            ptr = psum("ptr", [128, 512], BF16, P1)
            pv1 = psum("pv1", [128, 1], F32, P1)
            xacc = sbuf("xacc", [128, g.DC * NT1], F32, P1)
            dum1 = sbuf("dum1", [128, 512], BF16, P1)

            lwk = load_wtile("g", wk_sb, wk, 0, g.D, chan="wk")
            lwv = load_wtile("g", wv_sb, wv, 0, g.D, chan="wv")

            tile_ss_end = {}
            for i, (t0, tw) in enumerate(g.ttiles(g.PREW)):
                buf = xs[i % 2]
                w = [("ss", tile_ss_end.get(i - 2, 0))]
                ld = sb.dma("g", buf[:, :g.DC * tw].rearrange("p (dc x) -> p dc x", dc=g.DC),
                            wslice3(xpre, t0, tw), waits=w, chan="xs%d" % (i % 2))
                for dc in range(g.DC):
                    sb.op("s", (lambda buf=buf, dc=dc, tw=tw, i=i: lambda s: s.activation(
                        dum1[:, :tw], buf[:, dc * tw:dc * tw + tw], AF.Copy,
                        accum_out=xacc[:, dc * NT1 + i: dc * NT1 + i + 1]))(),
                        waits=[ld] if dc == 0 else (),
                        inc="ss" if dc == g.DC - 1 else None)
                tile_ss_end[i] = sb.c("ss")
            npre = len(g.ttiles(g.PREW))

            tile_ts_end = {}
            tile_vs_end2 = {}
            evac_bank = {}
            st_dma = {"k0": ("ts", 0), "k1": ("ts", 0), "v0": ("ts", 0), "v1": ("ts", 0)}
            tr_ts = {0: 0, 1: 0}
            for i, (t0, tw) in enumerate(g.ttiles(g.KBUF)):
                buf = xs[i % 2]
                prev_s = (tile_ss_end[npre + i - 2] if i >= 2
                          else tile_ss_end.get(npre - 1, 0))
                wv_ = [("ss", prev_s), ("ts", tile_ts_end.get(i - 2, 0))]
                ld = sb.dma("g", buf[:, :g.DC * tw].rearrange("p (dc x) -> p dc x", dc=g.DC),
                            wslice3(xkv, t0, tw), waits=wv_, chan="xs%d" % (i % 2))
                emit_convs(1)
                for dc in range(g.DC):
                    sb.op("s", (lambda buf=buf, dc=dc, tw=tw, i=i: lambda s: s.activation(
                        dum1[:, :tw], buf[:, dc * tw:dc * tw + tw], AF.Copy,
                        accum_out=xacc[:, dc * NT1 + npre + i: dc * NT1 + npre + i + 1]))(),
                        waits=[ld] if dc == 0 else (),
                        inc="ss" if dc == g.DC - 1 else None)
                tile_ss_end[npre + i] = sb.c("ss")
                for oc in range(2 * g.DC):
                    isv = oc >= g.DC
                    oc2 = oc - g.DC if isv else oc
                    wsb = wv_sb if isv else wk_sb
                    b = oc % 2
                    pairs = [(wsb[:, dc * g.D + oc2 * 128: dc * g.D + oc2 * 128 + 128],
                              buf[:, dc * tw: dc * tw + tw]) for dc in range(g.DC)]
                    tl = mm_acc(sb, pp[b][:, :tw], pairs,
                                waits=[ld, lwk, lwv, ("vs", evac_bank.get(b, 0))])
                    if not isv:
                        ev = sb.op("v", (lambda b=b, tw=tw: lambda v: v.tensor_copy(
                            kst[b][:, :tw], pp[b][:, :tw]))(),
                            waits=[("ts", tl), st_dma["k%d" % b]], inc="vs")
                        evac_bank[b] = ev
                        st_dma["k%d" % b] = sb.dma(
                            "y", k0d[oc2 * 128:(oc2 + 1) * 128, t0:t0 + tw], kst[b][:, :tw],
                            waits=[("vs", ev)], chan="kst%d" % b)
                    else:
                        ev = sb.op("v", (lambda b=b, tw=tw: lambda v: v.tensor_copy(
                            vst[b][:, :tw], pp[b][:, :tw]))(),
                            waits=[("ts", tl), ("ts", tr_ts[b])], inc="vs")
                        evac_bank[b] = ev
                        nz = tw // 128
                        for z in range(nz):
                            sb.op("t", (lambda b=b, z=z: lambda t: t.transpose(
                                ptr[:, z * 128:(z + 1) * 128],
                                vst[b][:, z * 128:(z + 1) * 128], identb[:]))(),
                                waits=[("vs", ev)] if z == 0 else (),
                                inc="ts" if z == nz - 1 else None)
                        tr_ts[b] = sb.c("ts")
                        vz = sb.op("v", (lambda b=b, tw=tw: lambda v: v.tensor_copy(
                            vtt[b][:, :tw], ptr[:, :tw]))(),
                            waits=[("ts", tr_ts[b]), st_dma["v%d" % b]], inc="vs")
                        st_dma["v%d" % b] = sb.dma(
                            "y",
                            v0td[oc2 * 128:(oc2 + 1) * 128, t0:t0 + tw],
                            vtt[b][:, :tw],
                            waits=[("vs", vz)], chan="vtt%d" % b)
                tile_ts_end[i] = sb.c("ts")

            allS = sb.c("ss")
            for dc in range(g.DC):
                sb.op("v", (lambda dc=dc: lambda v: v.reduce_sum(
                    xsum[:, dc:dc + 1], xacc[:, dc * NT1:(dc + 1) * NT1], axis=AX.X))(),
                    waits=[("ss", allS)] if dc == 0 else (), inc="vs")
            cv = sb.op("v", lambda v: v.tensor_copy(xsumb[:], xsum[:]), inc="vs", strict=True)
            for vc in range(g.DC):
                pairs = [(wv_sb[:, dc * g.D + vc * 128: dc * g.D + vc * 128 + 128],
                          xsumb[:, dc:dc + 1]) for dc in range(g.DC)]
                tl = mm_acc(sb, pv1[:], pairs, waits=[("vs", cv)])
                cv = sb.op("v", (lambda vc=vc: lambda v: v.tensor_copy(
                    vsum[:, vc:vc + 1], pv1[:]))(), waits=[("ts", tl)], inc="vs", strict=True)

        sb.fence()
        with contextlib.ExitStack() as P1b:
            wq_sb = sbuf("wq_sb", [128, g.DC * g.D], BF16, P1b)
            xsq = [sbuf(f"xq{i}", [128, g.DC * 512], BF16, P1b) for i in range(2)]
            qst = [sbuf(f"qst{i}", [128, 512], BF16, P1b) for i in range(2)]
            ppq = [psum(f"ppq{i}", [128, 512], F32, P1b) for i in range(2)]
            lwq = load_wtile("g", wq_sb, wq, 0, g.D, chan="wk")
            evb = {}
            qdma = {0: ("ts", 0), 1: ("ts", 0)}
            t_end = {}
            for i, (t0, tw) in enumerate(g.ttiles(g.QTOK)):
                buf = xsq[i % 2]
                ld = sb.dma("g", buf[:, :g.DC * tw].rearrange("p (dc x) -> p dc x", dc=g.DC),
                            wslice3(xkv, t0, tw), waits=[("ts", t_end.get(i - 2, 0))],
                            chan="xs%d" % (i % 2))
                emit_convs(1)
                for oc in range(g.DC):
                    b = oc % 2
                    pairs = [(wq_sb[:, dc * g.D + oc * 128: dc * g.D + oc * 128 + 128],
                              buf[:, dc * tw: dc * tw + tw]) for dc in range(g.DC)]
                    tl = mm_acc(sb, ppq[b][:, :tw], pairs,
                                waits=[ld, lwq, ("vs", evb.get(b, 0))])
                    ev = sb.op("v", (lambda b=b, tw=tw: lambda v: v.tensor_copy(
                        qst[b][:, :tw], ppq[b][:, :tw]))(),
                        waits=[("ts", tl), qdma[b]], inc="vs")
                    evb[b] = ev
                    qdma[b] = sb.dma("y", q0d[oc * 128:(oc + 1) * 128, t0:t0 + tw],
                                     qst[b][:, :tw], waits=[("vs", ev)], chan="qst%d" % b)
                t_end[i] = sb.c("ts")

        if stop_after == 1:
            sb.fence()
            if "k0" in dbg:
                sb.dma("y", dbg["k0"], k0d[:, :], chan="dbg")
            if "v0t" in dbg:
                sb.dma("y", dbg["v0t"], v0td[:, :], chan="dbg")
            if "q0" in dbg:
                sb.dma("y", dbg["q0"], q0d[:, :], chan="dbg")

        if stop_after is None or stop_after >= 2:
            P27 = top.enter_context(contextlib.ExitStack())
            stream = sbuf("stream", [128, g.DC * g.QTOK], BF16, P27)

        def emit_ln(sb, W, zget, nz, put_out, fp32out, pp_, denom):
            psz, psq, bcm, bcr = pp_["psz"], pp_["psq"], pp_["bcm"], pp_["bcr"]
            sq, t1 = pp_["sq"], pp_["t1"]
            onesl = ones_cf if fp32out else ones_cb
            onesr = ones_rf if fp32out else ones_rb
            prev_mm = 0
            for oc in range(nz):
                zap = zget(oc)
                sw = sb.op("s", (lambda zap=zap, W=W, sq=sq: lambda s: s.activation(
                    sq[:, :W], zap, AF.Square))(),
                    waits=[("vs", pp_["zready"][oc]), ("ts", prev_mm)], inc="ss")
                for idx, (ps_, rhs_, w_) in enumerate((
                        (psz, zap, [("vs", pp_["zready"][oc])] +
                         ([("vs", pp_["bc_free"])] if oc == 0 else [])),
                        (psq, sq[:, :W], [("ss", sw)]))):
                    def f(eng, ps_=ps_, rhs_=rhs_, oc=oc, W=W, onesl=onesl, nz=nz):
                        return eng.matmul(ps_[:, :W], onesl[:], rhs_,
                                          start=(oc == 0), stop=(oc == nz - 1))
                    sb.op("t", f, waits=w_, inc="ts")
                prev_mm = sb.c("ts")
            last = sb.c("ts")
            sb.op("v", (lambda W=W, d=denom, psz=psz: lambda v: v.tensor_scalar(
                rowsB[:, :W], psz[0:1, :W], 1.0 / d, None, ALU.mult))(),
                waits=[("ts", last)], inc="vs", strict=True)
            sb.op("v", (lambda W=W, d=denom, psq=psq: lambda v: v.tensor_scalar(
                rowsC[:, :W], psq[0:1, :W], 1.0 / d, None, ALU.mult))(), inc="vs", strict=True)
            m2v = sb.op("s", (lambda W=W: lambda s: s.activation(
                rowsA[0:1, :W], rowsB[:, :W], AF.Square))(),
                waits=[("vs", sb.c("vs"))], inc="ss", strict=True)
            sb.op("v", (lambda W=W: lambda v: v.tensor_tensor(
                rowsC[:, :W], rowsC[:, :W], rowsA[0:1, :W], ALU.subtract))(),
                waits=[("ss", m2v)], inc="vs", strict=True)
            sqv = sb.op("s", (lambda W=W: lambda s: s.activation(
                rowsC[:, :W], rowsC[:, :W], AF.Ln, bias=epsA[:]))(),
                waits=[("vs", sb.c("vs"))], inc="ss", strict=True)
            sqv = sb.op("s", (lambda W=W: lambda s: s.activation(
                rowsC[:, :W], rowsC[:, :W], AF.Exp, scale=-0.5))(), inc="ss", strict=True)
            rv = sb.op("v", (lambda W=W: lambda v: v.tensor_copy(rowsC[:, :W], rowsC[:, :W]))(),
                       waits=[("ss", sqv)], inc="vs", strict=True)
            if not fp32out:
                sb.op("v", (lambda W=W: lambda v: v.tensor_copy(rowsBF[0:1, :W], rowsB[:, :W]))(), inc="vs",
                      strict=True)
                rv = sb.op("v", (lambda W=W: lambda v: v.tensor_copy(rowsBF2[0:1, :W], rowsC[:, :W]))(), inc="vs",
                           strict=True)
                mrow, rrow = rowsBF, rowsBF2
            else:
                mrow, rrow = rowsB, rowsC
            t_m = mm_acc(sb, bcm[:, :W], [(onesr[:], mrow[0:1, :W])], waits=[("vs", rv)])
            t_r = mm_acc(sb, bcr[:, :W], [(onesr[:], rrow[0:1, :W])])
            for oc in range(nz):
                zap = zget(oc)
                if fp32out:
                    sb.op("v", (lambda zap=zap, W=W, bcm=bcm: lambda v: v.tensor_tensor(
                        zap, zap, bcm[:, :W], ALU.subtract))(),
                        waits=[("ts", t_r)], inc="vs", strict=True)
                    put_out(oc, None, bcr, W)
                else:
                    sb.op("v", (lambda zap=zap, W=W, t1=t1, bcm=bcm: lambda v: v.tensor_tensor(
                        t1[:, :W], zap, bcm[:, :W], ALU.subtract))(),
                        waits=[("ts", t_r)], inc="vs", strict=True)
                    put_out(oc, t1, bcr, W)
            pp_["bc_free"] = sb.c("vs")

        sb.phase = "P2"
        if stop_after is None or stop_after >= 2:
            sb.fence()
            with contextlib.ExitStack() as P2:
                kh = [sbuf(f"kh{i}", [128, g.KBUF], BF16, P2) for i in range(2)]
                vth = [sbuf(f"vth{i}", [128, g.KT * 128], BF16, P2) for i in range(2)]
                qh_s = sbuf("qh_s", [128, g.QTOK], BF16, P2)
                qh = [qh_s, qh_s]
                pt = [sbuf(f"pt{i}", [128, 512], BF16, P2) for i in range(6)]
                oh = sbuf("oh", [128, g.QTOK], BF16, P2)
                sums0 = sbuf("sums0", [1, g.QTOK], F32, P2)

                tmpO = sbuf("tmpO", [128, 512], BF16, P2)
                bias68 = sbuf("bias68_s", [128, g.KT], F32, P2)
                l0diag = sbuf("l0diag_s", [128, 4 * 512], BF16, P2)
                islast = sbuf("islast_s", [128, 512], BF16, P2)
                sp = [psum(f"sp{i}", [128, 1024], F32, P2) for i in range(2)]
                op_ = psum("op_", [128, 512], F32, P2)
                srow = psum("srow", [1, 512], F32, P2)
                bcp = psum("bcp", [128, 512], F32, P2)

                sb.dma("y", bias68[:], bias68I, chan="cst")
                sb.dma("y", l0diag[:], l0diagI, chan="cst")
                cst2 = sb.dma("y", islast[:], islastI, chan="cst")
                ldh = {}
                head_ts_end = {-1: 0, 0: 0}

                def load_head(h):
                    emit_convs(2)
                    i = h % 2
                    w = [("ts", head_ts_end.get(h - 2, 0))]
                    t1_ = sb.dma("y", kh[i][:], k0d[h * 128:(h + 1) * 128, :], waits=w,
                                 chan="kh%d" % i)
                    t2_ = sb.dma("y", vth[i][:], v0td[h * 128:(h + 1) * 128, :],
                                 waits=w, chan="vh%d" % i)
                    ldh[h] = [t1_, t2_, cst2]

                load_head(0)
                load_head(1)
                qchunks = g.ttiles(g.QTOK, 512)
                LAG = 2
                exp_tok = {}
                con_tok = {}
                msk_tok = {}
                Gbase = [0]
                evac_v = [0]
                for h in range(g.H):
                    i = h % 2
                    tq_ = sb.dma("y", qh_s[:], q0d[h * 128:(h + 1) * 128, :],
                                 waits=[("ts", head_ts_end.get(h - 1, 0))], chan="qh0")
                    ldh[h] = ldh[h] + [tq_]
                    groups = []
                    for ci, (q0c, w_) in enumerate(qchunks):
                        kstart = q0c // 128
                        n_t = g.KT - kstart
                        ngr = (n_t + 1) // 2
                        for gi in range(ngr):
                            kts = [kstart + 2 * gi + j for j in range(2)
                                   if 2 * gi + j < n_t]
                            groups.append((kts, q0c, kstart, gi, n_t, gi == ngr - 1))
                    NG = len(groups)
                    G0 = Gbase[0]

                    def emit_sp(G, h=h, i=i, groups=groups, G0=G0):
                        kts, q0c, kstart, gi, n_t, last_c = groups[G - G0]
                        b = G % 2
                        wS = list(ldh[h])
                        if G >= 2:
                            wS.append(("ss", exp_tok[G - 2]))
                        for j, kt in enumerate(kts):
                            sb.op("t", (lambda i=i, kt=kt, q0c=q0c, b=b, j=j: lambda t:
                                  t.matmul(sp[b][:, j * 512:(j + 1) * 512],
                                           kh[i][:, kt * 128:kt * 128 + 128],
                                           qh[i][:, q0c:q0c + 512], start=True, stop=True))(),
                                  waits=wS if j == 0 else (),
                                  inc="ts" if j == len(kts) - 1 else None)
                        tS = sb.c("ts")
                        wE = [("ts", tS)]
                        if G >= 3:
                            wE.append(("ts", con_tok[G - 3]))
                        for j, kt in enumerate(kts):
                            pb = (2 * G + j) % 6
                            sb.op("s", (lambda b=b, j=j, pb=pb, kt=kt: lambda s:
                                  s.activation(pt[pb][:, :512], sp[b][:, j * 512:(j + 1) * 512],
                                               AF.Exp, bias=bias68[:, kt:kt + 1],
                                               scale=g.SC))(),
                                  waits=wE if j == 0 else (),
                                  inc="ss" if j == len(kts) - 1 else None)
                        exp_tok[G] = sb.c("ss")
                        dep, dep_eng = exp_tok[G], "ss"
                        for j, kt in enumerate(kts):
                            d = kt - kstart
                            if d < 4:
                                pb = (2 * G + j) % 6
                                dv = sb.op("v", (lambda pb=pb, d=d: lambda v:
                                           v.tensor_tensor(pt[pb][:, :512], pt[pb][:, :512],
                                                           l0diag[:, d * 512:(d + 1) * 512],
                                                           ALU.mult))(),
                                           waits=[("ss", exp_tok[G])], inc="vs")
                                dep, dep_eng = dv, "vs"
                        msk_tok[G] = (dep_eng, dep)

                    def emit_cons(G, h=h, i=i, groups=groups, G0=G0):
                        kts, q0c, kstart, gi, n_t, last_c = groups[G - G0]
                        first_w = [msk_tok[G]]
                        if gi == 0:
                            first_w.append(("vs", evac_v[0]))
                        for j, kt in enumerate(kts):
                            pb = (2 * G + j) % 6
                            idx = 2 * gi + j
                            sb.op("t", (lambda pb=pb, first=(idx == 0),
                                        last=(idx == n_t - 1): lambda t:
                                  t.matmul(srow[:], ones_cb[:], pt[pb][:, :512],
                                           start=first, stop=last))(),
                                  waits=first_w if j == 0 else ())
                            sb.op("t", (lambda i=i, kt=kt, pb=pb, first=(idx == 0),
                                        last=(idx == n_t - 1): lambda t:
                                  t.matmul(op_[:, :512], vth[i][:, kt * 128:kt * 128 + 128],
                                           pt[pb][:, :512], start=first, stop=last))(),
                                  inc="ts" if j == len(kts) - 1 else None)
                        con_tok[G] = sb.c("ts")
                        if last_c:
                            sb.op("v", (lambda q0c=q0c, tk=con_tok[G]: lambda v: v.tensor_copy(
                                sums0[0:1, q0c:q0c + 512], srow[:]))(),
                                waits=[("ts", con_tok[G])], inc="vs")
                            evac_v[0] = sb.op("v", (lambda q0c=q0c: lambda v: v.tensor_copy(
                                oh[:, q0c:q0c + 512], op_[:, :512]))(), inc="vs")

                    for G in range(G0, G0 + NG):
                        emit_sp(G)
                        if G - G0 >= LAG:
                            emit_cons(G - LAG)
                    for G in range(G0 + NG - LAG, G0 + NG):
                        emit_cons(G)
                    Gbase[0] = G0 + NG
                    head_ts_end[h] = sb.c("ts")
                    if h + 1 < g.H:
                        load_head(h + 1)
                    f0 = g.QTOK - 512
                    sb.op("v", (lambda f0=f0: lambda v: v.tensor_tensor(
                        rowsA[0:1, :512], sums0[0:1, f0:], islast[0:1, :], ALU.mult))(),
                        waits=[("vs", sb.c("vs"))], inc="vs")
                    sb.op("v", (lambda f0=f0: lambda v: v.tensor_tensor(
                        sums0[0:1, f0:], sums0[0:1, f0:], rowsA[0:1, :512],
                        ALU.subtract))(), inc="vs", strict=True)
                    sfix = sb.op("s", (lambda: lambda s: s.mul(
                        rowsA[0:1, :512], islast[0:1, :], float(g.S)))(),
                        waits=[("vs", sb.c("vs"))], inc="ss")
                    sb.op("v", (lambda f0=f0: lambda v: v.tensor_tensor(
                        sums0[0:1, f0:], sums0[0:1, f0:], rowsA[0:1, :512], ALU.add))(),
                        waits=[("ss", sfix)], inc="vs", strict=True)
                    sb.op("v", (lambda h=h: lambda v: v.tensor_scalar(
                        tmpO[:, :], islast[:, :], vsum[:, h:h + 1], None, ALU.mult))(), inc="vs",
                        strict=True)
                    sb.op("v", (lambda f0=f0: lambda v: v.tensor_tensor(
                        oh[:, f0:], oh[:, f0:], tmpO[:, :], ALU.add))(), inc="vs", strict=True)
                    for q0c, w_ in qchunks:
                        sb.op("v", (lambda q0c=q0c: lambda v: v.reciprocal(
                            rowsA[0:1, :512], sums0[0:1, q0c:q0c + 512]))(), inc="vs",
                            strict=True)
                        tB = mm_acc(sb, bcp[:], [(ones_rf[:], rowsA[0:1, :512])],
                                    waits=[("vs", sb.c("vs"))])
                        sb.op("v", (lambda h=h, q0c=q0c: lambda v: v.tensor_tensor(
                            stream[:, h * g.QTOK + q0c: h * g.QTOK + q0c + 512],
                            oh[:, q0c:q0c + 512], bcp[:], ALU.mult))(),
                            waits=[("ts", tB)], inc="vs", strict=True)

            if stop_after == 2 and "stream" in dbg:
                sb.fence()
                for dc in range(g.DC):
                    sb.dma("y", dbg["stream"][dc * 128:(dc + 1) * 128, :],
                           stream[:, dc * g.QTOK:(dc + 1) * g.QTOK], chan="dbg")

        sb.phase = "P3"
        if stop_after is None or stop_after >= 3:
            sb.fence()
            with contextlib.ExitStack() as P3:
                wo_sb = sbuf("wo_sb", [128, g.DC * g.D], BF16, P3)
                xres_s = sbuf("xres_s", [128, g.DC * 512], BF16, P3)
                xres = [xres_s, xres_s]
                zbuf3v = sbuf("zbuf3v3", [128, g.DC * 512], BF16, P3)
                sq3 = sbuf("sq3", [128, 512], BF16, P3)
                t13 = sbuf("t13", [128, 512], F32, P3)
                zp = [psum(f"zp{i}", [128, 512], F32, P3) for i in range(2)]
                pp3 = {"psz": psum("psz3", [1, 512], F32, P3),
                       "psq": psum("psq3", [1, 512], F32, P3),
                       "bcm": psum("bcm3", [128, 512], F32, P3),
                       "bcr": psum("bcr3", [128, 512], F32, P3),
                       "sq": sq3, "t1": t13, "zready": {}, "bc_free": 0}
                lwo = load_wtile("g", wo_sb, wo, 0, g.D, chan="wk")
                sup_vs_end = {}
                for si, (s0, swd) in enumerate(g.ttiles(g.QTOK)):
                    ss_guard = sb.c("ss")
                    ldx = sb.dma("g", xres[si % 2][:, :g.DC * swd].rearrange(
                        "p (dc x) -> p dc x", dc=g.DC), wslice3(xkv, s0, swd),
                        waits=[("vs", sup_vs_end.get(si - 1, 0))], chan="xs%d" % (si % 2))
                    emit_convs(1)
                    zready = {}
                    for oc in range(g.DC):
                        pairs = [(wo_sb[:, dc * g.D + oc * 128: dc * g.D + oc * 128 + 128],
                                  stream[:, dc * g.QTOK + s0: dc * g.QTOK + s0 + swd])
                                 for dc in range(g.DC)]
                        tl = mm_acc(sb, zp[oc % 2][:, :swd], pairs,
                                    waits=[lwo, ldx, ("vs", zready.get(oc - 2, 0))])
                        zready[oc] = sb.op("v", (lambda oc=oc, i=si % 2, swd=swd: lambda v:
                                           v.tensor_tensor(zbuf3v[:, oc * 512: oc * 512 + swd],
                                                           zp[oc % 2][:, :swd],
                                                           xres[i][:, oc * swd: oc * swd + swd],
                                                           ALU.add))(),
                                           waits=[("ts", tl), ("ss", ss_guard)], inc="vs")
                    pp3["zready"] = zready
                    sup_vs_end[si] = sb.c("vs")

                    def put_out(oc, t1_, bcr_, W, s0=s0):
                        sb.op("v", (lambda oc=oc, W=W, s0=s0, t1_=t1_, bcr_=bcr_: lambda v:
                               v.tensor_tensor(stream[:, oc * g.QTOK + s0: oc * g.QTOK + s0 + W],
                                               t1_[:, :W], bcr_[:, :W], ALU.mult))(), inc="vs",
                               strict=True)

                    emit_ln(sb, swd, lambda oc, swd=swd: zbuf3v[:, oc * 512: oc * 512 + swd],
                            g.DC, put_out, False, pp3, g.D)

            if stop_after == 3 and "stream" in dbg:
                sb.fence()
                for dc in range(g.DC):
                    sb.dma("y", dbg["stream"][dc * 128:(dc + 1) * 128, :],
                           stream[:, dc * g.QTOK:(dc + 1) * g.QTOK], chan="dbg")

        sb.phase = "P4"
        if stop_after is None or stop_after >= 4:
            for nm in ("g0T", "u0T", "d0T"):
                ensure_conv(nm)
            sb.fence()
            if "streamA" in dbg:
                for dc_ in range(g.DC):
                    sb.dma("y", dbg["streamA"][dc_ * 128:(dc_ + 1) * 128, :],
                           stream[:, dc_ * g.QTOK:(dc_ + 1) * g.QTOK], chan="dbg")
            if "g0c" in dbg:
                sb.dma("y", dbg["g0c"], cache["g0T"][:, :], waits=conv_done["g0T"], chan="dbg")
            if "u0c" in dbg:
                sb.dma("y", dbg["u0c"], cache["u0T"][:, :], waits=conv_done["u0T"], chan="dbg")
            with contextlib.ExitStack() as P4:
                hdn = sbuf("hdn", [128, g.IC0 * 512], BF16, P4)
                wgt = [sbuf(f"wgt{i}", [128, g.DC * 128], BF16, P4) for i in range(2)]
                wut = [sbuf(f"wut{i}", [128, g.DC * 128], BF16, P4) for i in range(2)]
                wdt = [sbuf(f"wdt{i}", [128, g.IC0 * 128], BF16, P4) for i in range(2)]
                sg_t = sbuf("sg_t", [128, 512], BF16, P4)
                su_t = sbuf("su_t", [128, 512], BF16, P4)
                zbuf4v = sbuf("zbuf4v4", [128, g.DC * 512], BF16, P4)
                ysc = sbuf("ysc", [128, 512], F32, P4)
                sq4 = sbuf("sq4", [128, 512], BF16, P4)
                t14 = sbuf("t14", [128, 512], F32, P4)
                gs_sb = sbuf("gs_sb", [128, g.IC0], F32, P4)
                us_sb = sbuf("us_sb", [128, g.IC0], F32, P4)
                ds_sb = sbuf("ds_sb", [128, g.DC], F32, P4)
                pg = psum("pg", [128, 512], F32, P4)
                pu = psum("pu", [128, 512], F32, P4)
                py = [psum(f"py{i}", [128, 512], F32, P4) for i in range(2)]
                pp4 = {"psz": psum("psz4", [1, 512], F32, P4),
                       "psq": psum("psq4", [1, 512], F32, P4),
                       "bcm": psum("bcm4", [128, 512], F32, P4),
                       "bcr": psum("bcr4", [128, 512], F32, P4),
                       "sq": sq4, "t1": t14, "zready": {}, "bc_free": 0}
                sb.dma("y", gs_sb[:], gsI, chan="cst")
                sb.dma("y", us_sb[:], usI, chan="cst")
                cst4 = sb.dma("y", ds_sb[:], dsI, chan="cst")
                gu_t = {}
                dn_t, dn_v = {}, {}
                prev_sv = prev_hv = 0
                for si, (s0, swd) in enumerate(g.ttiles(g.QTOK)):
                    ss_guard = sb.c("ss")
                    hdn_guard = sb.c("ts")
                    emit_convs(3)
                    for ic in range(g.IC0):
                        ib = ic % 2
                        gidx = si * g.IC0 + ic
                        lg = load_wtile("y", wgt[ib], cache["g0T"], ic * 128, 128,
                                        waits=[("ts", gu_t.get(gidx - 2, 0))] + conv_done["g0T"],
                                        chan="wg%d" % ib)
                        lu = load_wtile("y", wut[ib], cache["u0T"], ic * 128, 128,
                                        waits=conv_done["u0T"], chan="wu%d" % ib)
                        pairs_g = [(wgt[ib][:, dc * 128: dc * 128 + 128],
                                    stream[:, dc * g.QTOK + s0: dc * g.QTOK + s0 + swd])
                                   for dc in range(g.DC)]
                        pairs_u = [(wut[ib][:, dc * 128: dc * 128 + 128],
                                    stream[:, dc * g.QTOK + s0: dc * g.QTOK + s0 + swd])
                                   for dc in range(g.DC)]
                        tg = mm_acc(sb, pg[:, :swd], pairs_g,
                                    waits=[lg, lu, cst4, ("ss", prev_sv), ("vs", prev_hv)])
                        tu = mm_acc(sb, pu[:, :swd], pairs_u)
                        gu_t[gidx] = tu
                        if si == 0 and ic == 0 and "pg0" in dbg:
                            dv = sb.op("v", (lambda swd=swd: lambda v: v.tensor_copy(
                                sq4[:, :swd], pg[:, :swd]))(), waits=[("ts", tg)], inc="vs")
                            sb.dma("y", dbg["pg0"][:, :swd], sq4[:, :swd], waits=[("vs", dv)], chan="dbg")
                        sv = sb.op("s", (lambda ic=ic, swd=swd: lambda s: s.activation(
                            sg_t[:, :swd], pg[:, :swd], AF.Silu, scale=gs_sb[:, ic:ic + 1]))(),
                            waits=[("ts", tg)], inc="ss")
                        prev_sv = sv
                        sb.op("v", (lambda ic=ic, swd=swd: lambda v: v.tensor_scalar(
                            su_t[:, :swd], pu[:, :swd], us_sb[:, ic:ic + 1], None, ALU.mult))(),
                            waits=[("ts", tu)], inc="vs")
                        if si == 0 and ic == 0 and "sg0" in dbg:
                            sb.dma("y", dbg["sg0"][:, :swd], sg_t[:, :swd], waits=[("ss", sv)], chan="dbg")
                            sb.dma("y", dbg["su0"][:, :swd], su_t[:, :swd], waits=[("vs", sb.c("vs"))], chan="dbg")
                        prev_hv = sb.op("v", (lambda ic=ic, swd=swd: lambda v:
                                        v.tensor_tensor(hdn[:, ic * 512: ic * 512 + swd],
                                                        sg_t[:, :swd], su_t[:, :swd],
                                                        ALU.mult))(),
                                        waits=[("ss", sv), ("ts", hdn_guard)], inc="vs")
                    hready = sb.c("vs")
                    if si == 0 and "hdn" in dbg:
                        sb.dma("y", dbg["hdn"][:, :], hdn[:, :], waits=[("vs", hready)], chan="dbg")
                    zready = {}
                    for oc in range(g.DC):
                        didx = si * g.DC + oc
                        ld = load_wtile("y", wdt[oc % 2], cache["d0T"], oc * 128, 128,
                                        waits=[("ts", dn_t.get(didx - 2, 0))] + conv_done["d0T"],
                                        chan="wd%d" % (oc % 2))
                        pairs = [(wdt[oc % 2][:, ic * 128: ic * 128 + 128],
                                  hdn[:, ic * 512: ic * 512 + swd]) for ic in range(g.IC0)]
                        tl = mm_acc(sb, py[oc % 2][:, :swd], pairs,
                                    waits=[ld, ("vs", hready), ("vs", dn_v.get(oc - 2, 0))])
                        dn_t[didx] = tl
                        sb.op("v", (lambda oc=oc, swd=swd: lambda v: v.tensor_scalar(
                            ysc[:, :swd], py[oc % 2][:, :swd], ds_sb[:, oc:oc + 1], None,
                            ALU.mult))(), waits=[("ts", tl)], inc="vs")
                        zv = sb.op("v", (lambda oc=oc, swd=swd, s0=s0: lambda v: v.tensor_tensor(
                            zbuf4v[:, oc * 512: oc * 512 + swd], ysc[:, :swd],
                            stream[:, oc * g.QTOK + s0: oc * g.QTOK + s0 + swd], ALU.add))(),
                            waits=[("ss", ss_guard)], inc="vs")
                        zready[oc] = zv
                        dn_v[oc] = zv
                    pp4["zready"] = zready

                    def put_out(oc, t1_, bcr_, W, s0=s0):
                        sb.op("v", (lambda oc=oc, W=W, s0=s0, t1_=t1_, bcr_=bcr_: lambda v:
                               v.tensor_tensor(stream[:, oc * g.QTOK + s0: oc * g.QTOK + s0 + W],
                                               t1_[:, :W], bcr_[:, :W], ALU.mult))(), inc="vs",
                               strict=True)

                    emit_ln(sb, swd, lambda oc, swd=swd: zbuf4v[:, oc * 512: oc * 512 + swd],
                            g.DC, put_out, False, pp4, g.D)

            if stop_after == 4 and "stream" in dbg:
                sb.fence()
                for dc in range(g.DC):
                    sb.dma("y", dbg["stream"][dc * 128:(dc + 1) * 128, :],
                           stream[:, dc * g.QTOK:(dc + 1) * g.QTOK])

        sb.phase = "P5"
        if stop_after is None or stop_after >= 5:
            sb.fence()
            with contextlib.ExitStack() as P5:
                Asb2 = [sbuf(f"Asb{i}", [128, g.QTOK], BF16, P5) for i in range(2)]
                Bsb2 = [sbuf(f"Bsb{i}", [128, g.QTOK], BF16, P5) for i in range(2)]
                sc1 = sbuf("sc1", [128, g.QTOK], F32, P5)
                sc2 = sbuf("sc2", [128, g.QTOK], F32, P5)
                sc3 = sbuf("sc3", [128, g.QTOK], F32, P5)
                sc4 = sbuf("sc4", [128, g.QTOK], F32, P5)
                cosb = sbuf("cosb", [128, g.QTOK], F32, P5)
                sinb = sbuf("sinb", [128, g.QTOK], F32, P5)
                wt5 = [sbuf(f"wt5{i}", [128, g.DC * 128], BF16, P5) for i in range(2)]
                vst5 = sbuf("vst5", [128, 512], BF16, P5)
                vtt5 = sbuf("vtt5", [128, 512], BF16, P5)
                pq5 = [psum(f"pq5{i}", [128, 512], F32, P5) for i in range(2)]
                ptr5 = psum("ptr5", [128, 512], BF16, P5)

                sb.dma("y", cosb[:], ropecI, chan="cst")
                tbl = sb.dma("y", sinb[:], ropesI, chan="cst")
                bank_ev = {0: 0, 1: 0}
                wt_ts = {}

                def proj(ob, dst_sb, a0, alen, extra=()):
                    lw = load_wtile("g", wt5[ob % 2], wqkvT, ob * 128, 128,
                                    waits=[("ts", wt_ts.get(ob - 2, 0))], chan="wt%d" % (ob % 2))
                    evl = 0
                    for (t0, tw) in g.ttiles(alen):
                        b = (t0 // 512) % 2
                        pairs = [(wt5[ob % 2][:, dc * 128: dc * 128 + 128],
                                  stream[:, dc * g.QTOK + a0 + t0: dc * g.QTOK + a0 + t0 + tw])
                                 for dc in range(g.DC)]
                        tl = mm_acc(sb, pq5[b][:, :tw], pairs,
                                    waits=[lw, ("vs", bank_ev[b])] + list(extra))
                        evl = sb.op("v", (lambda b=b, t0=t0, tw=tw, dst_sb=dst_sb: lambda v:
                                    v.tensor_copy(dst_sb[:, t0:t0 + tw], pq5[b][:, :tw]))(),
                                    waits=[("ts", tl)], inc="vs")
                        bank_ev[b] = evl
                    wt_ts[ob] = sb.c("ts")
                    return evl

                sb.phase = "P5a"
                spills_par = {0: [], 1: []}
                spill_i = [0]
                pi = 0
                for grp, base in enumerate((0, g.DC)):
                    isq = base == 0
                    a0 = g.HALO if isq else 0
                    alen = g.CH if isq else g.QTOK
                    dstd = qrd if isq else krd
                    for p in range(g.H // 2):
                        emit_convs(2)
                        Asb = Asb2[pi % 2]
                        Bsb = Bsb2[pi % 2]
                        guard = list(spills_par[pi % 2])
                        obA, obB = base + 2 * p, base + 2 * p + 1
                        evA = proj(obA, Asb, a0, alen, extra=guard)
                        evB = proj(obB, Bsb, a0, alen, extra=guard)
                        cs = cosb[:, a0:a0 + alen]
                        sn = sinb[:, a0:a0 + alen]
                        w0 = [("vs", max(evA, evB)), tbl]
                        sb.op("v", (lambda cs=cs, L=alen, Asb=Asb: lambda v: v.tensor_tensor(
                            sc1[:, :L], Asb[:, :L], cs, ALU.mult))(), waits=w0, inc="vs")
                        sb.op("v", (lambda sn=sn, L=alen, Asb=Asb: lambda v: v.tensor_tensor(
                            sc2[:, :L], Asb[:, :L], sn, ALU.mult))(), inc="vs")
                        sb.op("v", (lambda cs=cs, L=alen, Bsb=Bsb: lambda v: v.tensor_tensor(
                            sc3[:, :L], Bsb[:, :L], cs, ALU.mult))(), inc="vs")
                        sb.op("v", (lambda sn=sn, L=alen, Bsb=Bsb: lambda v: v.tensor_tensor(
                            sc4[:, :L], Bsb[:, :L], sn, ALU.mult))(), inc="vs")
                        sb.op("v", (lambda L=alen, Asb=Asb: lambda v: v.tensor_tensor(
                            Asb[:, :L], sc1[:, :L], sc4[:, :L], ALU.subtract))(), inc="vs",
                            strict=True)
                        rp = sb.op("v", (lambda L=alen, Bsb=Bsb: lambda v: v.tensor_tensor(
                            Bsb[:, :L], sc3[:, :L], sc2[:, :L], ALU.add))(), inc="vs",
                            strict=True)
                        h0, h1 = 2 * p, 2 * p + 1
                        toks = []
                        for hh, src_ in ((h0, Asb[0:64, :alen]), (h1, Asb[64:128, :alen]),
                                         (h0 + g.H, Bsb[0:64, :alen]), (h1 + g.H, Bsb[64:128, :alen])):
                            ro = (hh % g.H) * 128 + (64 if hh >= g.H else 0)
                            ch = "sp%d" % (spill_i[0] % 4)
                            spill_i[0] += 1
                            toks.append(sb.dma("y", dstd[ro: ro + 64, :], src_,
                                               waits=[("vs", rp)], chan=ch))
                        spills_par[pi % 2] = toks
                        pi += 1
                sb.phase = "P5b"
                v_dma = ("ts", 0)
                tr5_ts = 0
                for vb in range(g.DC):
                    emit_convs(2)
                    ob = 2 * g.DC + vb
                    lw = load_wtile("g", wt5[ob % 2], wqkvT, ob * 128, 128,
                                    waits=[("ts", wt_ts.get(ob - 2, 0))], chan="wt%d" % (ob % 2))
                    for (t0, tw) in g.ttiles(g.QTOK):
                        b = (t0 // 512) % 2
                        pairs = [(wt5[ob % 2][:, dc * 128: dc * 128 + 128],
                                  stream[:, dc * g.QTOK + t0: dc * g.QTOK + t0 + tw])
                                 for dc in range(g.DC)]
                        tl = mm_acc(sb, pq5[b][:, :tw], pairs,
                                    waits=[lw, ("vs", bank_ev[b])])
                        ev = sb.op("v", (lambda b=b, tw=tw: lambda v: v.tensor_copy(
                            vst5[:, :tw], pq5[b][:, :tw]))(),
                            waits=[("ts", tl), ("ts", tr5_ts)], inc="vs")
                        bank_ev[b] = ev
                        for z in range(tw // 128):
                            sb.op("t", (lambda z=z: lambda t: t.transpose(
                                ptr5[:, z * 128:(z + 1) * 128],
                                vst5[:, z * 128:(z + 1) * 128], identb[:]))(),
                                waits=[("vs", ev)] if z == 0 else (),
                                inc="ts" if z == tw // 128 - 1 else None)
                        tr5_ts = sb.c("ts")
                        vz = sb.op("v", (lambda tw=tw: lambda v: v.tensor_copy(
                            vtt5[:, :tw], ptr5[:, :tw]))(),
                            waits=[("ts", tr5_ts), v_dma], inc="vs")
                        v_dma = sb.dma(
                            "y",
                            v1td[vb * 128:(vb + 1) * 128, t0:t0 + tw],
                            vtt5[:, :tw],
                            waits=[("vs", vz)], chan="vt5")
                    wt_ts[ob] = sb.c("ts")

            if stop_after == 5:
                sb.fence()
                if "qr" in dbg:
                    sb.dma("y", dbg["qr"], qrd[:, :], chan="dbg")
                if "kr" in dbg:
                    sb.dma("y", dbg["kr"], krd[:, :], chan="dbg")
                if "v1t" in dbg:
                    sb.dma("y", dbg["v1t"], v1td[:, :], chan="dbg")

        sb.phase = "P6"
        if stop_after is None or stop_after >= 6:
            sb.fence()
            P67 = top.enter_context(contextlib.ExitStack())
            attn1 = sbuf("attn1", [128, g.DC * g.CH], BF16, P67)
            with contextlib.ExitStack() as P6:
                krh = sbuf("krh", [128, g.QTOK], BF16, P6)
                v1th = sbuf("v1th", [128, g.QT * 128], BF16, P6)
                qrh = sbuf("qrh", [128, g.CH], BF16, P6)
                pt6 = [sbuf(f"pt6{i}", [128, 512], BF16, P6) for i in range(6)]
                oh6 = sbuf("oh6", [128, g.CH], BF16, P6)
                sums6 = sbuf("sums6", [1, g.CH], F32, P6)

                l1m = sbuf("l1m_s", [128, (8 + g.HALO // 128) * 512], BF16, P6)
                sp6 = [psum(f"sp6{i}", [128, 1024], F32, P6) for i in range(2)]
                op6 = psum("op6", [128, 512], F32, P6)
                srow6 = psum("srow6", [1, 512], F32, P6)
                bcp6 = psum("bcp6", [128, 512], F32, P6)

                cst6 = sb.dma("y", l1m[:], l1mI, chan="cst")
                LAG6 = 2
                exp6_tok, con6_tok, msk6_tok = {}, {}, {}
                G6base = [0]
                evac6_v = [0]
                h6_end = {}
                h6_mark = {}
                for h in range(g.H):
                    emit_convs(2)
                    kb6, vb6, qb6 = krh, v1th, qrh
                    rows = slice(h * 128, (h + 1) * 128)
                    wA = [("ts", h6_mark.get(h - 1, 0))]
                    wB = [("ts", h6_end.get(h - 1, 0))]
                    tkA1 = sb.dma("y", kb6[:, 0:1024], krd[rows, 0:1024], waits=wA, chan="l1k")
                    tkA2 = sb.dma("y", vb6[:, 0:1024], v1td[rows, 0:1024], waits=wA, chan="l1v")
                    tkA3 = sb.dma("y", qb6[:, 0:1024], qrd[rows, 0:1024], waits=wA, chan="l1q")
                    tkB1 = sb.dma("y", kb6[:, 1024:g.QTOK], krd[rows, 1024:g.QTOK],
                                  waits=wB, chan="l1k")
                    tkB2 = sb.dma("y", vb6[:, 1024:g.QT * 128], v1td[rows, 1024:g.QT * 128],
                                  waits=wB, chan="l1v")
                    tkB3 = sb.dma("y", qb6[:, 1024:g.CH], qrd[rows, 1024:g.CH],
                                  waits=wB, chan="l1q")
                    ldqA = [tkA1, tkA2, tkA3, cst6]
                    ldqB = [tkB1, tkB2, tkB3]
                    groups = []
                    for ci in range(g.CH // 512):
                        qa = g.HALO + 512 * ci
                        lo_t = max(0, (qa - g.W + 1) // 128)
                        hi_t = (qa + 511) // 128
                        n_t = hi_t - lo_t + 1
                        ngr = (n_t + 1) // 2
                        for gi in range(ngr):
                            kts = [lo_t + 2 * gi + j for j in range(2) if 2 * gi + j < n_t]
                            groups.append((kts, ci, lo_t, gi, n_t, gi == ngr - 1))
                    NG = len(groups)
                    G0 = G6base[0]

                    def emit_sp6(G, groups=groups, G0=G0, ldqA=ldqA, ldqB=ldqB,
                                 kb6=kb6, qb6=qb6):
                        kts, ci, lo_t, gi, n_t, last_c = groups[G - G0]
                        b = G % 2
                        wS = list(ldqA) if ci == 0 else list(ldqA + ldqB)
                        if G >= 2:
                            wS.append(("ss", exp6_tok[G - 2]))
                        for j, kt in enumerate(kts):
                            sb.op("t", (lambda kt=kt, ci=ci, b=b, j=j, kb6=kb6, qb6=qb6: lambda t:
                                  t.matmul(sp6[b][:, j * 512:(j + 1) * 512],
                                           kb6[:, kt * 128:kt * 128 + 128],
                                           qb6[:, 512 * ci:512 * ci + 512],
                                           start=True, stop=True))(),
                                  waits=wS if j == 0 else (),
                                  inc="ts" if j == len(kts) - 1 else None)
                        tS = sb.c("ts")
                        wE = [("ts", tS)]
                        if G >= 3:
                            wE.append(("ts", con6_tok[G - 3]))
                        for j, kt in enumerate(kts):
                            pb = (2 * G + j) % 6
                            sb.op("s", (lambda b=b, j=j, pb=pb: lambda s: s.activation(
                                pt6[pb][:, :512], sp6[b][:, j * 512:(j + 1) * 512],
                                AF.Exp, scale=g.SC))(),
                                waits=wE if j == 0 else (),
                                inc="ss" if j == len(kts) - 1 else None)
                        exp6_tok[G] = sb.c("ss")
                        for j, kt in enumerate(kts):
                            pb = (2 * G + j) % 6
                            delta = kt - lo_t
                            mi = delta if kt >= g.HALO // 128 else 8 + kt
                            sb.op("v", (lambda pb=pb, mi=mi: lambda v: v.tensor_tensor(
                                pt6[pb][:, :512], pt6[pb][:, :512],
                                l1m[:, mi * 512:(mi + 1) * 512], ALU.mult))(),
                                waits=[("ss", exp6_tok[G])] if j == 0 else (),
                                inc="vs" if j == len(kts) - 1 else None)
                        msk6_tok[G] = sb.c("vs")

                    def emit_cons6(G, groups=groups, G0=G0, vb6=vb6):
                        kts, ci, lo_t, gi, n_t, last_c = groups[G - G0]
                        first_w = [("vs", msk6_tok[G])]
                        if gi == 0:
                            first_w.append(("vs", evac6_v[0]))
                        for j, kt in enumerate(kts):
                            pb = (2 * G + j) % 6
                            idx = 2 * gi + j
                            sb.op("t", (lambda pb=pb, first=(idx == 0),
                                        last=(idx == n_t - 1): lambda t:
                                  t.matmul(srow6[:], ones_cb[:], pt6[pb][:, :512],
                                           start=first, stop=last))(),
                                  waits=first_w if j == 0 else ())
                            sb.op("t", (lambda kt=kt, pb=pb, first=(idx == 0),
                                        last=(idx == n_t - 1), vb6=vb6: lambda t: t.matmul(
                                op6[:, :512], vb6[:, kt * 128:kt * 128 + 128],
                                pt6[pb][:, :512], start=first, stop=last))(),
                                inc="ts" if j == len(kts) - 1 else None)
                        con6_tok[G] = sb.c("ts")
                        if last_c:
                            sb.op("v", (lambda ci=ci, tk=con6_tok[G]: lambda v: v.tensor_copy(
                                sums6[0:1, 512 * ci:512 * ci + 512], srow6[:]))(),
                                waits=[("ts", con6_tok[G])], inc="vs")
                            evac6_v[0] = sb.op("v", (lambda ci=ci: lambda v: v.tensor_copy(
                                oh6[:, 512 * ci:512 * ci + 512], op6[:, :512]))(),
                                inc="vs")

                    for G in range(G0, G0 + NG):
                        emit_sp6(G)
                        if G - G0 >= LAG6:
                            emit_cons6(G - LAG6)
                            if (G - LAG6) - G0 == 7:
                                h6_mark[h] = sb.c("ts")
                    for G in range(G0 + NG - LAG6, G0 + NG):
                        emit_cons6(G)
                    G6base[0] = G0 + NG
                    for ci in range(g.CH // 512):
                        sb.op("v", (lambda ci=ci: lambda v: v.reciprocal(
                            rowsA[0:1, :512], sums6[0:1, 512 * ci:512 * ci + 512]))(),
                            waits=[("vs", sb.c("vs"))], inc="vs")
                        tB = mm_acc(sb, bcp6[:], [(ones_rf[:], rowsA[0:1, :512])],
                                    waits=[("vs", sb.c("vs"))])
                        sb.op("v", (lambda h=h, ci=ci: lambda v: v.tensor_tensor(
                            attn1[:, h * g.CH + 512 * ci: h * g.CH + 512 * ci + 512],
                            oh6[:, 512 * ci:512 * ci + 512], bcp6[:], ALU.mult))(),
                            waits=[("ts", tB)], inc="vs", strict=True)
                    h6_end[h] = sb.c("ts")

            if stop_after == 6 and "attn1" in dbg:
                sb.fence()
                for dc in range(g.DC):
                    sb.dma("y", dbg["attn1"][dc * 128:(dc + 1) * 128, :],
                           attn1[:, dc * g.CH:(dc + 1) * g.CH], chan="dbg")

        sb.phase = "P7"
        if stop_after is None or stop_after >= 7:
            ensure_conv("woutT")
            sb.fence()
            with contextlib.ExitStack() as P7:
                wot = [sbuf(f"wot{i}", [128, g.DC * 128], BF16, P7) for i in range(2)]
                zbuf7v = sbuf("zbuf7v7", [128, g.DC * 512], BF16, P7)
                sq7 = sbuf("sq7", [128, 512], BF16, P7)
                t17 = sbuf("t17", [128, 512], F32, P7)
                zp7 = [psum(f"zp7{i}", [128, 512], F32, P7) for i in range(2)]
                pp7 = {"psz": psum("psz7", [1, 512], F32, P7),
                       "psq": psum("psq7", [1, 512], F32, P7),
                       "bcm": psum("bcm7", [128, 512], F32, P7),
                       "bcr": psum("bcr7", [128, 512], F32, P7),
                       "sq": sq7, "t1": t17, "zready": {}, "bc_free": 0}
                wo_ts = {}
                x3_dma = {}
                for si, (s0, swd) in enumerate(g.ttiles(g.CH)):
                    ss_guard = sb.c("ss")
                    zready = {}
                    for oc in range(g.DC):
                        widx = si * g.DC + oc
                        lw = load_wtile("y", wot[oc % 2], cache["woutT"], oc * 128, 128,
                                        waits=[("ts", wo_ts.get(widx - 2, 0))] + conv_done["woutT"],
                                        chan="wg%d" % (oc % 2))
                        pairs = [(wot[oc % 2][:, dc * 128: dc * 128 + 128],
                                  attn1[:, dc * g.CH + s0: dc * g.CH + s0 + swd])
                                 for dc in range(g.DC)]
                        tl = mm_acc(sb, zp7[oc % 2][:, :swd], pairs,
                                    waits=[lw, ("vs", zready.get(oc - 2, 0))])
                        wo_ts[widx] = tl
                        zready[oc] = sb.op("v", (lambda oc=oc, swd=swd, s0=s0: lambda v:
                                           v.tensor_tensor(zbuf7v[:, oc * 512: oc * 512 + swd],
                                                           zp7[oc % 2][:, :swd],
                                                           stream[:, oc * g.QTOK + g.HALO + s0:
                                                                  oc * g.QTOK + g.HALO + s0 + swd],
                                                           ALU.add))(),
                                           waits=[("ts", tl), ("ss", ss_guard)], inc="vs")
                    pp7["zready"] = zready

                    def put_out(oc, t1_, bcr_, W, s0=s0, si=si):
                        wv = sb.op("v", (lambda oc=oc, W=W, t1_=t1_, bcr_=bcr_: lambda v:
                                   v.tensor_tensor(zbuf7v[:, oc * 512: oc * 512 + W],
                                                   t1_[:, :W], bcr_[:, :W], ALU.mult))(),
                                   waits=[x3_dma.get((si - 1, oc), ("ts", 0))], inc="vs",
                                   strict=True)
                        x3_dma[(si, oc)] = sb.dma(
                            "y", x3d[oc * 128:(oc + 1) * 128, s0:s0 + W],
                            zbuf7v[:, oc * 512: oc * 512 + W], waits=[("vs", wv)],
                            chan="x3%d" % (oc % 2))

                    emit_ln(sb, swd, lambda oc, swd=swd: zbuf7v[:, oc * 512: oc * 512 + swd],
                            g.DC, put_out, False, pp7, g.D)
                    pp7["bc_free"] = sb.c("vs")

            if stop_after == 7 and "x3" in dbg:
                sb.fence()
                sb.dma("y", dbg["x3"], x3d[:, :], chan="dbg")

        sb.phase = "P8"
        if stop_after is None or stop_after >= 8:
            if stop_after is None or stop_after >= 6:
                P67.close()
            P27.close()
            for nm in ("w1g", "w1u", "w1d"):
                ensure_conv(nm)
            sb.fence()
            with contextlib.ExitStack() as P8:
                hdn1 = sbuf("hdn1", [128, g.IC1 * 512], BF16, P8)
                x3sl2 = [sbuf(f"x3sl{i}", [128, g.DC * 512], BF16, P8) for i in range(2)]
                wg1 = [sbuf(f"wg1{i}", [128, g.DC * 128], BF16, P8) for i in range(2)]
                wu1 = [sbuf(f"wu1{i}", [128, g.DC * 128], BF16, P8) for i in range(2)]
                wd1 = [sbuf(f"wd1{i}", [128, g.IC1 * 128], BF16, P8) for i in range(2)]
                sg1 = sbuf("sg1", [128, 512], BF16, P8)
                zf = sbuf("zf", [128, g.DC * 512], F32, P8)
                am = [sbuf(f"am{i}", [128, 512], BF16, P8) for i in range(4)]
                mx1 = sbuf("mx1", [128, 512], BF16, P8)
                mx2 = sbuf("mx2", [128, 512], BF16, P8)
                m2t = sbuf("m2t", [128, 512], BF16, P8)
                outT = sbuf("outT", [128, g.D // 2], F32, P8)
                sq8 = sbuf("sq8", [128, 512], F32, P8)
                t18 = sq8
                pg8 = psum("pg8", [128, 512], F32, P8)
                pu8 = psum("pu8", [128, 512], F32, P8)
                py8_ = psum("py8", [128, 512], F32, P8)
                py8 = [py8_, py8_]
                pp8 = {"psz": psum("psz8", [1, 512], F32, P8),
                       "psq": psum("psq8", [1, 512], F32, P8),
                       "bcm": psum("bcm8", [128, 512], F32, P8),
                       "bcr": psum("bcr8", [128, 512], F32, P8),
                       "sq": sq8, "t1": t18, "zready": {}, "bc_free": 0}
                ptr8 = psum("ptr8", [128, 512], F32, P8)

                gu1t = {}
                wd_ts8 = {}
                x_ts = {}
                zdone = {}
                pending = {}
                ydma_half = {}
                ydma_last = [("ts", 0)]
                prev_sv1 = prev_hv1 = 0
                sup8 = g.ttiles(g.CH)

                def ldx8(si):
                    s0_, swd_ = sup8[si]
                    return sb.dma(
                        "y", x3sl2[si % 2][:, :g.DC * swd_].rearrange(
                            "p (dc x) -> p dc x", dc=g.DC),
                        x3d.rearrange("(dc p) x -> p dc x", p=128)[:, :, s0_:s0_ + swd_],
                        waits=[("ts", x_ts.get(si - 2, 0)), ("vs", zdone.get(si - 2, 0))],
                        chan="x8")

                pending[0] = ldx8(0)
                for si, (s0, swd) in enumerate(sup8):
                    ss_guard = sb.c("ss")
                    hdn_guard = sb.c("ts")
                    if si + 1 < len(sup8):
                        pending[si + 1] = ldx8(si + 1)
                    ldx = pending[si]
                    x3sl = x3sl2[si % 2]
                    def sparsify(gb, swd=swd):
                        tiles = [hdn1[:, (m * g.GB1 + gb) * 512:(m * g.GB1 + gb) * 512 + swd]
                                 for m in range(4)]
                        gbv = sb.c("vs")
                        for m in range(4):
                            sb.op("s", (lambda m=m, tiles=tiles, swd=swd: lambda s: s.activation(
                                am[m][:, :swd], tiles[m], AF.Abs))(),
                                waits=[("vs", gbv)], inc="ss")
                        sw = sb.c("ss")
                        net = ((mx1, am[0], am[1], ALU.max, False),
                               (mx2, am[2], am[3], ALU.max, False),
                               (mx1, mx1, mx2, ALU.min, True),
                               (mx2, am[0], am[1], ALU.min, False),
                               (m2t, am[2], am[3], ALU.min, False),
                               (mx2, mx2, m2t, ALU.max, True),
                               (m2t, mx1, mx2, ALU.max, True))
                        for gi8, (o, a, b2, opk, st) in enumerate(net):
                            sb.op("v", (lambda o=o, a=a, b2=b2, opk=opk, swd=swd: lambda v:
                                   v.tensor_tensor(o[:, :swd], a[:, :swd], b2[:, :swd], opk))(),
                                   waits=[("ss", sw)] if gi8 == 0 else (), inc="vs", strict=st)
                        for m in range(4):
                            sb.op("v", (lambda m=m, swd=swd: lambda v: v.tensor_tensor(
                                am[m][:, :swd], am[m][:, :swd], m2t[:, :swd], ALU.is_ge))(),
                                inc="vs", strict=(m == 0))
                        for m in range(4):
                            sb.op("v", (lambda m=m, tiles=tiles, swd=swd: lambda v: v.tensor_tensor(
                                tiles[m], tiles[m], am[m][:, :swd], ALU.mult))(),
                                inc="vs", strict=(m == 0))

                    for ic in range(g.IC1):
                        ib = ic % 2
                        gidx = si * g.IC1 + ic
                        lg = load_wtile("y", wg1[ib], cache["w1g"], ic * 128, 128,
                                        waits=[("ts", gu1t.get(gidx - 2, 0))] + conv_done["w1g"],
                                        chan="wg%d" % ib)
                        lu = load_wtile("y", wu1[ib], cache["w1u"], ic * 128, 128,
                                        waits=conv_done["w1u"], chan="wu%d" % ib)
                        pairs_g = [(wg1[ib][:, dc * 128: dc * 128 + 128],
                                    x3sl[:, dc * swd: dc * swd + swd]) for dc in range(g.DC)]
                        pairs_u = [(wu1[ib][:, dc * 128: dc * 128 + 128],
                                    x3sl[:, dc * swd: dc * swd + swd]) for dc in range(g.DC)]
                        tg = mm_acc(sb, pg8[:, :swd], pairs_g,
                                    waits=[lg, lu, ldx,
                                           ("ss", prev_sv1), ("vs", prev_hv1)])
                        tu = mm_acc(sb, pu8[:, :swd], pairs_u)
                        gu1t[gidx] = tu
                        sv = sb.op("s", (lambda swd=swd: lambda s: s.activation(
                            sg1[:, :swd], pg8[:, :swd], AF.Silu))(), waits=[("ts", tg)], inc="ss")
                        prev_sv1 = sv
                        prev_hv1 = sb.op("v", (lambda ic=ic, swd=swd: lambda v:
                                         v.tensor_tensor(hdn1[:, ic * 512: ic * 512 + swd],
                                                         sg1[:, :swd], pu8[:, :swd],
                                                         ALU.mult))(),
                                         waits=[("ss", sv), ("ts", tu), ("ts", hdn_guard)],
                                         inc="vs")
                        if ic >= 3 * g.GB1:
                            sparsify(ic - 3 * g.GB1)
                    hready = sb.c("vs")
                    zready = {}
                    for oc in range(g.DC):
                        didx8 = si * g.DC + oc
                        ld = load_wtile("y", wd1[oc % 2], cache["w1d"], oc * 128, 128,
                                        waits=[("ts", wd_ts8.get(didx8 - 2, 0))] + conv_done["w1d"],
                                        chan="wd%d" % (oc % 2))
                        pairs = [(wd1[oc % 2][:, ic * 128: ic * 128 + 128],
                                  hdn1[:, ic * 512: ic * 512 + swd]) for ic in range(g.IC1)]
                        tl = mm_acc(sb, py8[oc % 2][:, :swd], pairs,
                                    waits=[ld, ("vs", hready), ("vs", zready.get(oc - 1, 0))])
                        wd_ts8[didx8] = tl
                        zready[oc] = sb.op("v", (lambda oc=oc, swd=swd, x3sl=x3sl: lambda v:
                            v.tensor_tensor(
                            zf[:, oc * 512: oc * 512 + swd], py8[oc % 2][:, :swd],
                            x3sl[:, oc * swd: oc * swd + swd], ALU.add))(),
                            waits=[("ts", tl), ("ss", ss_guard)], inc="vs")
                    pp8["zready"] = zready
                    zdone[si] = sb.c("vs")
                    x_ts[si] = sb.c("ts")

                    def put_out(oc, t1_, bcr_, W):
                        sb.op("v", (lambda oc=oc, W=W, bcr_=bcr_: lambda v:
                               v.tensor_tensor(zf[:, oc * 512: oc * 512 + W],
                                               zf[:, oc * 512: oc * 512 + W],
                                               bcr_[:, :W], ALU.mult))(), inc="vs",
                               strict=True)

                    emit_ln(sb, swd, lambda oc, swd=swd: zf[:, oc * 512: oc * 512 + swd],
                            g.DC, put_out, True, pp8, g.D)
                    lnv = sb.c("vs")
                    trv = 0
                    gsz = min(4, g.DC // 2)
                    for tb in range(swd // 128):
                        for hf in range(2):
                            for oc4 in range(hf * g.DC // 2, (hf + 1) * g.DC // 2, gsz):
                                for k4 in range(gsz):
                                    oc = oc4 + k4
                                    sb.op("t", (lambda oc=oc, tb=tb, k4=k4: lambda t: t.transpose(
                                        ptr8[:, k4 * 128:(k4 + 1) * 128],
                                        zf[:, oc * 512 + tb * 128: oc * 512 + tb * 128 + 128],
                                        identf[:]))(),
                                        waits=[("vs", lnv), ydma_half.get(hf, ("ts", 0)),
                                               ("vs", trv)] if k4 == 0 else (),
                                        inc="ts" if k4 == gsz - 1 else None)
                                tt4 = sb.c("ts")
                                trv = sb.op("v", (lambda oc4=oc4, hf=hf, gsz=gsz: lambda v:
                                    v.tensor_copy(
                                    outT[:, (oc4 - hf * g.DC // 2) * 128:
                                            (oc4 - hf * g.DC // 2 + gsz) * 128], ptr8[:, :gsz * 128]))(),
                                    waits=[("ts", tt4), ydma_last[0]], inc="vs")
                            ydma_last[0] = ydma_half[hf] = sb.dma(
                                "y", y[s0 + tb * 128: s0 + tb * 128 + 128,
                                       hf * g.D // 2:(hf + 1) * g.D // 2], outT[:, :g.D // 2],
                                waits=[("vs", trv)], chan="yo%d" % hf)
                    x_ts[si] = sb.c("ts")

        sb.fence()
        for k_, v_ in sb.cnt.items():
            assert v_ < 60000, ("semaphore overflow risk", k_, v_)
        nins = sum(len(q) for q in sb.q.values())
        sb.flush(blk)
        build_program.last_stats = {"cnt": dict(sb.cnt), "n_ops": nins,
                                    "n_chans": len(sb.chans), "marks": sb.marks}

    return nc


def kernel(**inputs):
    g = G_FULL
    sh, cores = stage_host(g, inputs)
    nc = build_program(g)
    in_maps = [dict(sh, **cores[c]) for c in range(8)]
    res = run_bass_kernel_spmd(nc, in_maps, core_ids=list(range(8)))
    out = np.empty((g.B, g.S, g.D), dtype=np.float32)
    for c in range(8):
        b, j = c // 4, c % 4
        out[b, j * g.CH:(j + 1) * g.CH, :] = res.results[c]["y"]
    return out


# version 15
# speedup vs baseline: 1.0498x; 1.0498x over previous
import math
import numpy as np
import ml_dtypes
import contextlib
import concourse.bass as bass
import concourse.mybir as mybir
from concourse.bass_utils import run_bass_kernel_spmd

F32 = mybir.dt.float32
BF16 = mybir.dt.bfloat16
AF = mybir.ActivationFunctionType
ALU = mybir.AluOpType
AX = mybir.AxisListType
BFNP = ml_dtypes.bfloat16


class G:
    def __init__(self, S, D, I0, I1, W=512, HALO=512, B=2):
        self.B, self.S, self.D = B, S, D
        self.I0, self.I1, self.W, self.HALO = I0, I1, W, HALO
        self.DC = D // 128
        self.H = self.DC
        self.CH = S // 4
        self.KBUF = S + HALO
        self.KT = self.KBUF // 128
        self.QTOK = self.CH + HALO
        self.QT = self.QTOK // 128
        self.IC0 = I0 // 128
        self.IC1 = I1 // 128
        self.GB1 = self.IC1 // 4
        self.PREW = 3 * self.CH - HALO
        self.SC = 1.0 / math.sqrt(128.0)
        assert D % 128 == 0 and self.CH % 512 == 0 and HALO % 128 == 0
        assert self.KBUF % 128 == 0 and I0 % 128 == 0 and I1 % 512 == 0
        assert self.PREW % 512 == 0 and self.QTOK % 512 == 0
        assert self.W % 128 == 0 and self.HALO >= self.W - 1

    def ttiles(self, total, w=512):
        out, t = [], 0
        while t < total:
            tw = min(w, total - t)
            out.append((t, tw))
            t += tw
        return out


G_FULL = G(S=8192, D=2048, I0=5632, I1=8192)


def stage_host(g, ins):
    f32 = np.float32
    xT = np.ascontiguousarray(np.transpose(np.asarray(ins["L0_attn_x"]), (0, 2, 1))).astype(f32, copy=False)
    sh = {}
    sh["wq"] = np.ascontiguousarray(np.asarray(ins["L0_attn_Wq"], dtype=f32))
    sh["wk"] = np.ascontiguousarray(np.asarray(ins["L0_attn_Wk"], dtype=f32))
    sh["wv"] = np.ascontiguousarray(np.asarray(ins["L0_attn_Wv"], dtype=f32))
    sh["wo"] = np.ascontiguousarray(np.asarray(ins["L0_attn_Wo"], dtype=f32))
    sh["g0T"] = np.ascontiguousarray(np.asarray(ins["L0_ffn_gate_w"], dtype=f32).T)
    sh["u0T"] = np.ascontiguousarray(np.asarray(ins["L0_ffn_up_w"], dtype=f32).T)
    sh["d0T"] = np.ascontiguousarray(np.asarray(ins["L0_ffn_down_w"], dtype=f32).T)
    sh["gs"] = np.ascontiguousarray(np.asarray(ins["L0_ffn_gate_s"], dtype=f32).reshape(g.IC0, 128).T)
    sh["us"] = np.ascontiguousarray(np.asarray(ins["L0_ffn_up_s"], dtype=f32).reshape(g.IC0, 128).T)
    sh["ds"] = np.ascontiguousarray(np.asarray(ins["L0_ffn_down_s"], dtype=f32).reshape(g.DC, 128).T)

    wqkvT = np.asarray(ins["L1_attn_w_qkv"], dtype=f32).T
    cols = []
    for base in (0, g.D):
        for p in range(g.H // 2):
            h0, h1 = 2 * p, 2 * p + 1
            cols.extend(range(base + 128 * h0, base + 128 * h0 + 64))
            cols.extend(range(base + 128 * h1, base + 128 * h1 + 64))
            cols.extend(range(base + 128 * h0 + 64, base + 128 * h0 + 128))
            cols.extend(range(base + 128 * h1 + 64, base + 128 * h1 + 128))
    cols.extend(range(2 * g.D, 3 * g.D))
    sh["wqkvT"] = np.ascontiguousarray(wqkvT[:, cols])
    sh["woutT"] = np.ascontiguousarray(np.asarray(ins["L1_attn_w_out"], dtype=f32).T)

    n = np.arange(g.I1)
    c_, p_ = n // 128, n % 128
    perm1 = 4 * ((c_ % g.GB1) * 128 + p_) + c_ // g.GB1
    sh["w1g"] = np.ascontiguousarray(np.asarray(ins["L1_ffn_w_gate"], dtype=f32)[:, perm1])
    sh["w1u"] = np.ascontiguousarray(np.asarray(ins["L1_ffn_w_up"], dtype=f32)[:, perm1])
    sh["w1d"] = np.ascontiguousarray(np.asarray(ins["L1_ffn_w_down"], dtype=f32)[perm1, :])

    sh["identb"] = np.eye(128, dtype=BFNP)
    sh["identf"] = np.eye(128, dtype=np.float32)
    dg = np.zeros((128, 4 * 512), dtype=BFNP)
    kk = np.arange(128)[:, None]
    qq = np.arange(512)[None, :]
    for d in range(4):
        dg[:, d * 512:(d + 1) * 512] = (128 * d + kk > qq).astype(BFNP)
    sh["l0diag"] = dg

    cores = []
    for c in range(8):
        b, j = c // 4, c % 4
        lo = g.CH * j - g.HALO
        xkv = np.zeros((g.D, g.KBUF), dtype=f32)
        s0 = max(lo, 0)
        xkv[:, s0 - lo:g.S - lo] = xT[b][:, s0:g.S]
        xpre = np.zeros((g.D, g.PREW), dtype=f32)
        if lo > 0:
            xpre[:, :lo] = xT[b][:, :lo]
        bias68 = np.zeros((128, g.KT), dtype=f32)
        valid = g.KBUF - g.CH * j
        tidx = np.arange(g.KT)[None, :] * 128 + np.arange(128)[:, None]
        bias68[tidx >= valid] = -30000.0
        nhalo = g.HALO // 128
        l1m = np.zeros((128, (8 + nhalo) * 512), dtype=BFNP)
        p = np.arange(128)[:, None]
        for d in range(8):
            diff = g.HALO + qq - 128 * d - p
            l1m[:, d * 512:(d + 1) * 512] = ((diff >= 0) & (diff < g.W)).astype(BFNP)
        for d in range(nhalo):
            m = np.array(l1m[:, d * 512:(d + 1) * 512])
            if j == 0:
                m[:] = 0
            l1m[:, (8 + d) * 512:(9 + d) * 512] = m
        islast = np.zeros((128, 512), dtype=BFNP)
        if j == 3:
            islast[:, 511] = 1
        pos = lo + np.arange(g.QTOK)[None, :]
        dfreq = (np.arange(128) % 64).astype(np.float64)
        invf = 1.0 / (10000.0 ** (2.0 * dfreq / 128.0))
        ang = pos * invf[:, None]
        cores.append({
            "xkv": xkv, "xpre": xpre, "bias68": bias68, "l1m": l1m,
            "islast": islast,
            "ropec": np.cos(ang).astype(f32), "ropes": np.sin(ang).astype(f32),
        })
    return sh, cores


ENG = {"t": "tensor", "s": "scalar", "v": "vector", "g": "gpsimd", "y": "sync"}


SERIAL = [False]
SERIAL_PHASES = set()
OWN_WAIT = [False]


class SB:
    def __init__(self):
        self.q = {e: [] for e in ENG}
        self.cnt = {k: 0 for k in ("ts", "ss", "vs", "gc")}
        self.sems = None
        self.chans = {}
        self.real_idx = {e: 0 for e in ENG}
        self.marks = {e: [] for e in ENG}

    def __setattr__(self, k, v):
        if k == "phase" and hasattr(self, "marks"):
            for e in ENG:
                self.marks[e].append((self.real_idx[e], v))
        object.__setattr__(self, k, v)

    OWN = {"v": "vs", "s": "ss", "g": "gc"}
    NPOOL = 88

    def chan_sem(self, chan):
        if chan not in self.chans:
            i = len(self.chans)
            assert i < self.NPOOL, "semaphore pool exhausted"
            self.chans[chan] = "c%d" % i
            self.cnt[self.chans[chan]] = 0
        return self.chans[chan]

    def op(self, e, fn, waits=(), inc=None, n=1, strict=False):
        waits = list(waits)
        own = self.OWN.get(e)
        if own is not None:
            if OWN_WAIT[0] or strict:
                waits.append((own, self.cnt[own]))
            if inc is None:
                inc = own
        if SERIAL[0] or getattr(self, "phase", None) in SERIAL_PHASES:
            waits = waits + [(k, v) for k, v in self.cnt.items() if v > 0]
        if fn is not None:
            self.real_idx[e] += 1
        self.q[e].append((tuple(waits), fn, inc, n))
        if inc is not None:
            self.cnt[inc] += n
            return self.cnt[inc]
        return None

    def dma(self, e, out, in_, waits=(), chan="misc"):
        sem = self.chan_sem(chan)
        gate = (sem, self.cnt[sem])
        self.op(e, (lambda out=out, in_=in_: (lambda eng: eng.dma_start(out=out, in_=in_)))(),
                waits=list(waits) + [gate], inc=sem, n=16)
        return (sem, self.cnt[sem])

    def fence(self, engines=None, sems=None):
        snap = [(k, v) for k, v in self.cnt.items() if v > 0 and (sems is None or k in sems)]
        for e in (engines or ENG):
            self.q[e].append((tuple(snap), None, None, 0))

    def c(self, sem):
        return self.cnt[sem]

    def flush(self, block):
        sems = self.sems
        for e, name in ENG.items():
            ops = self.q[e]
            if not ops:
                continue

            def body(eng, ops=ops):
                seen = {}
                for waits, fn, inc, n in ops:
                    for s, v in waits:
                        if v > 0 and seen.get(s, -1) < v:
                            eng.wait_ge(sems[s], v)
                            seen[s] = v
                    if fn is None:
                        continue
                    r = fn(eng)
                    if inc is not None:
                        r.then_inc(sems[inc], n)
            getattr(block, name)(body)
            self.q[e] = []


def mm_acc(sb, ps, pairs, waits=()):
    n = len(pairs)
    for i, (l, r) in enumerate(pairs):
        def f(eng, l=l, r=r, i=i, n=n, ps=ps):
            return eng.matmul(ps, l, r, start=(i == 0), stop=(i == n - 1))
        sb.op("t", f, waits=waits if i == 0 else (), inc="ts" if i == n - 1 else None)
    return sb.c("ts")


def build_program(g, stop_after=None, dump=()):
    nc = bass.Bass("TRN2", target_bir_lowering=False, debug=False)
    E = lambda n, s, d: nc.dram_tensor(n, s, d, kind="ExternalInput").ap()

    xkv = E("xkv", [g.D, g.KBUF], F32)
    xpre = E("xpre", [g.D, g.PREW], F32)
    wq, wk, wv, wo = (E(n, [g.D, g.D], F32) for n in ("wq", "wk", "wv", "wo"))
    g0T = E("g0T", [g.D, g.I0], F32)
    u0T = E("u0T", [g.D, g.I0], F32)
    d0T = E("d0T", [g.I0, g.D], F32)
    gsI = E("gs", [128, g.IC0], F32)
    usI = E("us", [128, g.IC0], F32)
    dsI = E("ds", [128, g.DC], F32)
    wqkvT = E("wqkvT", [g.D, 3 * g.D], F32)
    woutT = E("woutT", [g.D, g.D], F32)
    w1g = E("w1g", [g.D, g.I1], F32)
    w1u = E("w1u", [g.D, g.I1], F32)
    w1d = E("w1d", [g.I1, g.D], F32)
    bias68I = E("bias68", [128, g.KT], F32)
    l0diagI = E("l0diag", [128, 4 * 512], BF16)
    l1mI = E("l1m", [128, (8 + g.HALO // 128) * 512], BF16)
    islastI = E("islast", [128, 512], BF16)
    ropecI = E("ropec", [128, g.QTOK], F32)
    ropesI = E("ropes", [128, g.QTOK], F32)
    identbI = E("identb", [128, 128], BF16)
    identfI = E("identf", [128, 128], F32)

    y = nc.dram_tensor("y", [g.CH, g.D], F32, kind="ExternalOutput").ap()
    dbg = {}
    shapes = {"k0": [g.D, g.KBUF], "v0t": [g.H * g.KBUF, 128], "q0": [g.D, g.QTOK],
              "stream": [g.D, g.QTOK], "attn1": [g.D, g.CH], "x3": [g.D, g.CH],
              "qr": [g.D, g.CH], "kr": [g.D, g.QTOK], "v1t": [g.H * g.QTOK, 128],
              "hdn": [128, g.IC0 * 512], "hdn1": [128, g.IC1 * 512],
              "g0c": [g.D, g.I0], "u0c": [g.D, g.I0],
              "pg0": [128, 512], "sg0": [128, 512], "su0": [128, 512],
              "streamA": [g.D, g.QTOK]}
    for name in dump:
        dbg[name] = nc.dram_tensor("dbg_" + name, shapes[name], BF16, kind="ExternalOutput").ap()

    k0d = nc.dram_tensor("k0d", [g.D, g.KBUF], BF16).ap()
    v0td = nc.dram_tensor("v0td", [g.H * 128, g.KT * 128], BF16).ap()
    q0d = nc.dram_tensor("q0d", [g.D, g.QTOK], BF16).ap()
    qrd = nc.dram_tensor("qrd", [g.D, g.CH], BF16).ap()
    krd = nc.dram_tensor("krd", [g.D, g.QTOK], BF16).ap()
    v1td = nc.dram_tensor("v1td", [g.H * 128, g.QTOK], BF16).ap()
    x3d = nc.dram_tensor("x3d", [g.D, g.CH], BF16).ap()
    cache_src = {"g0T": g0T, "u0T": u0T, "d0T": d0T, "woutT": woutT,
                 "w1g": w1g, "w1u": w1u, "w1d": w1d}
    cache = {n: nc.dram_tensor(n + "_c", list(src.shape), BF16).ap()
             for n, src in cache_src.items()}
    conv_order = ["g0T", "u0T", "d0T", "woutT", "w1g", "w1u", "w1d"]
    conv_jobs = {}
    for nm in conv_order:
        src = cache_src[nm]
        R = src.shape[0]
        step = 512
        conv_jobs[nm] = [(cache[nm][r:r + min(step, R - r), :], src[r:r + min(step, R - r), :])
                         for r in range(0, R, step)]
    conv_done = {}

    sb = SB()

    with contextlib.ExitStack() as top:
        sems = {}
        for s in ("ts", "ss", "vs", "gc"):
            sems[s] = top.enter_context(nc.semaphore(s))
        for i in range(SB.NPOOL):
            sems["c%d" % i] = top.enter_context(nc.semaphore("c%d" % i))
        sb.sems = sems
        blk = top.enter_context(nc.Block())

        def sbuf(name, shape, dt, stack):
            return stack.enter_context(nc.sbuf_tensor(name, shape, dt))

        def psum(name, shape, dt, stack):
            return stack.enter_context(nc.psum_tensor(name, shape, dt))

        identb = sbuf("identb_s", [128, 128], BF16, top)
        identf = sbuf("identf_s", [128, 128], F32, top)
        ones_cb = sbuf("ones_cb", [128, 1], BF16, top)
        ones_cf = sbuf("ones_cf", [128, 1], F32, top)
        ones_rb = sbuf("ones_rb", [1, 128], BF16, top)
        ones_rf = sbuf("ones_rf", [1, 128], F32, top)
        xsum = sbuf("xsum", [128, g.DC], F32, top)
        xsumb = sbuf("xsumb", [128, g.DC], BF16, top)
        vsum = sbuf("vsum", [128, g.DC], F32, top)
        rowsA = sbuf("rowsA", [1, 1024], F32, top)
        rowsB = sbuf("rowsB", [1, 1024], F32, top)
        rowsC = sbuf("rowsC", [1, 1024], F32, top)
        rowsBF = sbuf("rowsBF", [1, 1024], BF16, top)
        rowsBF2 = sbuf("rowsBF2", [1, 1024], BF16, top)
        xr_t = sbuf("xr_t", [128, 1], F32, top)
        NT1 = len(g.ttiles(g.PREW)) + len(g.ttiles(g.KBUF))
        epsA = sbuf("epsA", [1, 1], F32, top)

        sb.dma("y", identb[:], identbI, chan="cst")
        sb.dma("y", identf[:], identfI, chan="cst")
        sb.op("v", lambda v: v.memset(ones_cb[:], 1.0), inc="vs")
        sb.op("v", lambda v: v.memset(ones_cf[:], 1.0), inc="vs")
        sb.op("v", lambda v: v.memset(ones_rb[:], 1.0), inc="vs")
        sb.op("v", lambda v: v.memset(ones_rf[:], 1.0), inc="vs")
        sb.op("v", lambda v: v.memset(xsum[:], 0.0), inc="vs")
        sb.op("v", lambda v: v.memset(epsA[:], 1e-5), inc="vs")

        def wslice3(src, a, bwidth):
            return src.rearrange("(rc p) x -> p rc x", p=128)[:, :, a:a + bwidth]

        def load_wtile(e, dst, src, a, bwidth, waits=(), chan="w"):
            nch = src.shape[0] // 128
            return sb.dma(e, dst[:, :nch * bwidth].rearrange("p (rc x) -> p rc x", rc=nch),
                          wslice3(src, a, bwidth), waits=waits, chan=chan)

        conv_flat = [(nm, j) for nm in conv_order for j in conv_jobs[nm]]
        conv_pos = [0]

        conv_tok = {}

        def emit_convs(k):
            while k > 0 and conv_pos[0] < len(conv_flat):
                nm, (dst, src) = conv_flat[conv_pos[0]]
                ch = "cv%d" % (conv_pos[0] % 4)
                conv_tok[ch] = sb.dma("g", dst, src, chan=ch)
                conv_done[nm] = list(conv_tok.values())
                conv_pos[0] += 1
                k -= 1

        def ensure_conv(nm):
            while any(n2 == nm for n2, _ in conv_flat[conv_pos[0]:]):
                emit_convs(1)
            return conv_done[nm]

        sb.phase = "P1"
        with contextlib.ExitStack() as P1:
            wk_sb = sbuf("wk_sb", [128, g.DC * g.D], BF16, P1)
            wv_sb = sbuf("wv_sb", [128, g.DC * g.D], BF16, P1)
            xs = [sbuf(f"xs{i}", [128, g.DC * 512], BF16, P1) for i in range(2)]
            kst = [sbuf(f"kst{i}", [128, 512], BF16, P1) for i in range(2)]
            vst = [sbuf(f"vst{i}", [128, 512], BF16, P1) for i in range(2)]
            vtt = [sbuf(f"vtt{i}", [128, 512], BF16, P1) for i in range(2)]
            pp = [psum(f"pp{i}", [128, 512], F32, P1) for i in range(2)]
            ptr = psum("ptr", [128, 512], BF16, P1)
            pv1 = psum("pv1", [128, 1], F32, P1)
            xacc = sbuf("xacc", [128, g.DC * NT1], F32, P1)
            dum1 = sbuf("dum1", [128, 512], BF16, P1)

            lwk = load_wtile("g", wk_sb, wk, 0, g.D, chan="wk")
            lwv = load_wtile("g", wv_sb, wv, 0, g.D, chan="wv")

            tile_ss_end = {}
            for i, (t0, tw) in enumerate(g.ttiles(g.PREW)):
                buf = xs[i % 2]
                w = [("ss", tile_ss_end.get(i - 2, 0))]
                ld = sb.dma("g", buf[:, :g.DC * tw].rearrange("p (dc x) -> p dc x", dc=g.DC),
                            wslice3(xpre, t0, tw), waits=w, chan="xs%d" % (i % 2))
                for dc in range(g.DC):
                    sb.op("s", (lambda buf=buf, dc=dc, tw=tw, i=i: lambda s: s.activation(
                        dum1[:, :tw], buf[:, dc * tw:dc * tw + tw], AF.Copy,
                        accum_out=xacc[:, dc * NT1 + i: dc * NT1 + i + 1]))(),
                        waits=[ld] if dc == 0 else (),
                        inc="ss" if dc == g.DC - 1 else None)
                tile_ss_end[i] = sb.c("ss")
            npre = len(g.ttiles(g.PREW))

            tile_ts_end = {}
            tile_vs_end2 = {}
            evac_bank = {}
            st_dma = {"k0": ("ts", 0), "k1": ("ts", 0), "v0": ("ts", 0), "v1": ("ts", 0)}
            tr_ts = {0: 0, 1: 0}
            for i, (t0, tw) in enumerate(g.ttiles(g.KBUF)):
                buf = xs[i % 2]
                prev_s = (tile_ss_end[npre + i - 2] if i >= 2
                          else tile_ss_end.get(npre - 1, 0))
                wv_ = [("ss", prev_s), ("ts", tile_ts_end.get(i - 2, 0))]
                ld = sb.dma("g", buf[:, :g.DC * tw].rearrange("p (dc x) -> p dc x", dc=g.DC),
                            wslice3(xkv, t0, tw), waits=wv_, chan="xs%d" % (i % 2))
                emit_convs(1)
                for dc in range(g.DC):
                    sb.op("s", (lambda buf=buf, dc=dc, tw=tw, i=i: lambda s: s.activation(
                        dum1[:, :tw], buf[:, dc * tw:dc * tw + tw], AF.Copy,
                        accum_out=xacc[:, dc * NT1 + npre + i: dc * NT1 + npre + i + 1]))(),
                        waits=[ld] if dc == 0 else (),
                        inc="ss" if dc == g.DC - 1 else None)
                tile_ss_end[npre + i] = sb.c("ss")
                for oc in range(2 * g.DC):
                    isv = oc >= g.DC
                    oc2 = oc - g.DC if isv else oc
                    wsb = wv_sb if isv else wk_sb
                    b = oc % 2
                    pairs = [(wsb[:, dc * g.D + oc2 * 128: dc * g.D + oc2 * 128 + 128],
                              buf[:, dc * tw: dc * tw + tw]) for dc in range(g.DC)]
                    tl = mm_acc(sb, pp[b][:, :tw], pairs,
                                waits=[ld, lwk, lwv, ("vs", evac_bank.get(b, 0))])
                    if not isv:
                        ev = sb.op("v", (lambda b=b, tw=tw: lambda v: v.tensor_copy(
                            kst[b][:, :tw], pp[b][:, :tw]))(),
                            waits=[("ts", tl), st_dma["k%d" % b]], inc="vs")
                        evac_bank[b] = ev
                        st_dma["k%d" % b] = sb.dma(
                            "y", k0d[oc2 * 128:(oc2 + 1) * 128, t0:t0 + tw], kst[b][:, :tw],
                            waits=[("vs", ev)], chan="kst%d" % b)
                    else:
                        ev = sb.op("v", (lambda b=b, tw=tw: lambda v: v.tensor_copy(
                            vst[b][:, :tw], pp[b][:, :tw]))(),
                            waits=[("ts", tl), ("ts", tr_ts[b])], inc="vs")
                        evac_bank[b] = ev
                        nz = tw // 128
                        for z in range(nz):
                            sb.op("t", (lambda b=b, z=z: lambda t: t.transpose(
                                ptr[:, z * 128:(z + 1) * 128],
                                vst[b][:, z * 128:(z + 1) * 128], identb[:]))(),
                                waits=[("vs", ev)] if z == 0 else (),
                                inc="ts" if z == nz - 1 else None)
                        tr_ts[b] = sb.c("ts")
                        vz = sb.op("v", (lambda b=b, tw=tw: lambda v: v.tensor_copy(
                            vtt[b][:, :tw], ptr[:, :tw]))(),
                            waits=[("ts", tr_ts[b]), st_dma["v%d" % b]], inc="vs")
                        st_dma["v%d" % b] = sb.dma(
                            "y",
                            v0td[oc2 * 128:(oc2 + 1) * 128, t0:t0 + tw],
                            vtt[b][:, :tw],
                            waits=[("vs", vz)], chan="vtt%d" % b)
                tile_ts_end[i] = sb.c("ts")

            allS = sb.c("ss")
            for dc in range(g.DC):
                sb.op("v", (lambda dc=dc: lambda v: v.reduce_sum(
                    xsum[:, dc:dc + 1], xacc[:, dc * NT1:(dc + 1) * NT1], axis=AX.X))(),
                    waits=[("ss", allS)] if dc == 0 else (), inc="vs")
            cv = sb.op("v", lambda v: v.tensor_copy(xsumb[:], xsum[:]), inc="vs", strict=True)
            for vc in range(g.DC):
                pairs = [(wv_sb[:, dc * g.D + vc * 128: dc * g.D + vc * 128 + 128],
                          xsumb[:, dc:dc + 1]) for dc in range(g.DC)]
                tl = mm_acc(sb, pv1[:], pairs, waits=[("vs", cv)])
                cv = sb.op("v", (lambda vc=vc: lambda v: v.tensor_copy(
                    vsum[:, vc:vc + 1], pv1[:]))(), waits=[("ts", tl)], inc="vs", strict=True)

        sb.fence()
        with contextlib.ExitStack() as P1b:
            wq_sb = sbuf("wq_sb", [128, g.DC * g.D], BF16, P1b)
            xsq = [sbuf(f"xq{i}", [128, g.DC * 512], BF16, P1b) for i in range(2)]
            qst = [sbuf(f"qst{i}", [128, 512], BF16, P1b) for i in range(2)]
            ppq = [psum(f"ppq{i}", [128, 512], F32, P1b) for i in range(2)]
            lwq = load_wtile("g", wq_sb, wq, 0, g.D, chan="wk")
            evb = {}
            qdma = {0: ("ts", 0), 1: ("ts", 0)}
            t_end = {}
            for i, (t0, tw) in enumerate(g.ttiles(g.QTOK)):
                buf = xsq[i % 2]
                ld = sb.dma("g", buf[:, :g.DC * tw].rearrange("p (dc x) -> p dc x", dc=g.DC),
                            wslice3(xkv, t0, tw), waits=[("ts", t_end.get(i - 2, 0))],
                            chan="xs%d" % (i % 2))
                emit_convs(1)
                for oc in range(g.DC):
                    b = oc % 2
                    pairs = [(wq_sb[:, dc * g.D + oc * 128: dc * g.D + oc * 128 + 128],
                              buf[:, dc * tw: dc * tw + tw]) for dc in range(g.DC)]
                    tl = mm_acc(sb, ppq[b][:, :tw], pairs,
                                waits=[ld, lwq, ("vs", evb.get(b, 0))])
                    ev = sb.op("v", (lambda b=b, tw=tw: lambda v: v.tensor_copy(
                        qst[b][:, :tw], ppq[b][:, :tw]))(),
                        waits=[("ts", tl), qdma[b]], inc="vs")
                    evb[b] = ev
                    qdma[b] = sb.dma("y", q0d[oc * 128:(oc + 1) * 128, t0:t0 + tw],
                                     qst[b][:, :tw], waits=[("vs", ev)], chan="qst%d" % b)
                t_end[i] = sb.c("ts")

        if stop_after == 1:
            sb.fence()
            if "k0" in dbg:
                sb.dma("y", dbg["k0"], k0d[:, :], chan="dbg")
            if "v0t" in dbg:
                sb.dma("y", dbg["v0t"], v0td[:, :], chan="dbg")
            if "q0" in dbg:
                sb.dma("y", dbg["q0"], q0d[:, :], chan="dbg")

        if stop_after is None or stop_after >= 2:
            P27 = top.enter_context(contextlib.ExitStack())
            stream = sbuf("stream", [128, g.DC * g.QTOK], BF16, P27)

        def emit_ln(sb, W, zget, nz, put_out, fp32out, pp_, denom):
            psz, psq, bcm, bcr = pp_["psz"], pp_["psq"], pp_["bcm"], pp_["bcr"]
            sq, t1 = pp_["sq"], pp_["t1"]
            onesl = ones_cf if fp32out else ones_cb
            onesr = ones_rf if fp32out else ones_rb
            prev_mm = 0
            for oc in range(nz):
                zap = zget(oc)
                sw = sb.op("s", (lambda zap=zap, W=W, sq=sq: lambda s: s.activation(
                    sq[:, :W], zap, AF.Square))(),
                    waits=[("vs", pp_["zready"][oc]), ("ts", prev_mm)], inc="ss")
                for idx, (ps_, rhs_, w_) in enumerate((
                        (psz, zap, [("vs", pp_["zready"][oc])] +
                         ([("vs", pp_["bc_free"])] if oc == 0 else [])),
                        (psq, sq[:, :W], [("ss", sw)]))):
                    def f(eng, ps_=ps_, rhs_=rhs_, oc=oc, W=W, onesl=onesl, nz=nz):
                        return eng.matmul(ps_[:, :W], onesl[:], rhs_,
                                          start=(oc == 0), stop=(oc == nz - 1))
                    sb.op("t", f, waits=w_, inc="ts")
                prev_mm = sb.c("ts")
            last = sb.c("ts")
            sb.op("v", (lambda W=W, d=denom, psz=psz: lambda v: v.tensor_scalar(
                rowsB[:, :W], psz[0:1, :W], 1.0 / d, None, ALU.mult))(),
                waits=[("ts", last)], inc="vs", strict=True)
            sb.op("v", (lambda W=W, d=denom, psq=psq: lambda v: v.tensor_scalar(
                rowsC[:, :W], psq[0:1, :W], 1.0 / d, None, ALU.mult))(), inc="vs", strict=True)
            m2v = sb.op("s", (lambda W=W: lambda s: s.activation(
                rowsA[0:1, :W], rowsB[:, :W], AF.Square))(),
                waits=[("vs", sb.c("vs"))], inc="ss", strict=True)
            sb.op("v", (lambda W=W: lambda v: v.tensor_tensor(
                rowsC[:, :W], rowsC[:, :W], rowsA[0:1, :W], ALU.subtract))(),
                waits=[("ss", m2v)], inc="vs", strict=True)
            sqv = sb.op("s", (lambda W=W: lambda s: s.activation(
                rowsC[:, :W], rowsC[:, :W], AF.Ln, bias=epsA[:]))(),
                waits=[("vs", sb.c("vs"))], inc="ss", strict=True)
            sqv = sb.op("s", (lambda W=W: lambda s: s.activation(
                rowsC[:, :W], rowsC[:, :W], AF.Exp, scale=-0.5))(), inc="ss", strict=True)
            rv = sb.op("v", (lambda W=W: lambda v: v.tensor_copy(rowsC[:, :W], rowsC[:, :W]))(),
                       waits=[("ss", sqv)], inc="vs", strict=True)
            if not fp32out:
                sb.op("v", (lambda W=W: lambda v: v.tensor_copy(rowsBF[0:1, :W], rowsB[:, :W]))(), inc="vs",
                      strict=True)
                rv = sb.op("v", (lambda W=W: lambda v: v.tensor_copy(rowsBF2[0:1, :W], rowsC[:, :W]))(), inc="vs",
                           strict=True)
                mrow, rrow = rowsBF, rowsBF2
            else:
                mrow, rrow = rowsB, rowsC
            t_m = mm_acc(sb, bcm[:, :W], [(onesr[:], mrow[0:1, :W])], waits=[("vs", rv)])
            t_r = mm_acc(sb, bcr[:, :W], [(onesr[:], rrow[0:1, :W])])
            for oc in range(nz):
                zap = zget(oc)
                if fp32out:
                    sb.op("v", (lambda zap=zap, W=W, bcm=bcm: lambda v: v.tensor_tensor(
                        zap, zap, bcm[:, :W], ALU.subtract))(),
                        waits=[("ts", t_r)], inc="vs", strict=True)
                    put_out(oc, None, bcr, W)
                else:
                    sb.op("v", (lambda zap=zap, W=W, t1=t1, bcm=bcm: lambda v: v.tensor_tensor(
                        t1[:, :W], zap, bcm[:, :W], ALU.subtract))(),
                        waits=[("ts", t_r)], inc="vs", strict=True)
                    put_out(oc, t1, bcr, W)
            pp_["bc_free"] = sb.c("vs")

        sb.phase = "P2"
        if stop_after is None or stop_after >= 2:
            sb.fence()
            with contextlib.ExitStack() as P2:
                kh = [sbuf(f"kh{i}", [128, g.KBUF], BF16, P2) for i in range(2)]
                vth = [sbuf(f"vth{i}", [128, g.KT * 128], BF16, P2) for i in range(2)]
                qh_s = sbuf("qh_s", [128, g.QTOK], BF16, P2)
                qh = [qh_s, qh_s]
                pt = [sbuf(f"pt{i}", [128, 512], BF16, P2) for i in range(6)]
                oh = sbuf("oh", [128, g.QTOK], BF16, P2)
                sums0 = sbuf("sums0", [1, g.QTOK], F32, P2)

                tmpO = sbuf("tmpO", [128, 512], BF16, P2)
                bias68 = sbuf("bias68_s", [128, g.KT], F32, P2)
                l0diag = sbuf("l0diag_s", [128, 4 * 512], BF16, P2)
                islast = sbuf("islast_s", [128, 512], BF16, P2)
                sp = [psum(f"sp{i}", [128, 1024], F32, P2) for i in range(2)]
                op_ = psum("op_", [128, 512], F32, P2)
                srow = psum("srow", [1, 512], F32, P2)
                bcp = psum("bcp", [128, 512], F32, P2)

                sb.dma("y", bias68[:], bias68I, chan="cst")
                sb.dma("y", l0diag[:], l0diagI, chan="cst")
                cst2 = sb.dma("y", islast[:], islastI, chan="cst")
                ldh = {}
                head_ts_end = {-1: 0, 0: 0}

                def load_head(h):
                    emit_convs(2)
                    i = h % 2
                    w = [("ts", head_ts_end.get(h - 2, 0))]
                    t1_ = sb.dma("y", kh[i][:], k0d[h * 128:(h + 1) * 128, :], waits=w,
                                 chan="kh%d" % i)
                    t2_ = sb.dma("y", vth[i][:], v0td[h * 128:(h + 1) * 128, :],
                                 waits=w, chan="vh%d" % i)
                    ldh[h] = [t1_, t2_, cst2]

                load_head(0)
                load_head(1)
                qchunks = g.ttiles(g.QTOK, 512)
                LAG = 2
                exp_tok = {}
                con_tok = {}
                msk_tok = {}
                Gbase = [0]
                evac_v = [0]
                qh_mark = {}
                for h in range(g.H):
                    i = h % 2
                    tqA = sb.dma("y", qh_s[:, 0:1024], q0d[h * 128:(h + 1) * 128, 0:1024],
                                 waits=[("ts", qh_mark.get(h - 1, 0))], chan="qh0")
                    tqB = sb.dma("y", qh_s[:, 1024:g.QTOK],
                                 q0d[h * 128:(h + 1) * 128, 1024:g.QTOK],
                                 waits=[("ts", head_ts_end.get(h - 1, 0))], chan="qh0")
                    groups = []
                    for ci, (q0c, w_) in enumerate(qchunks):
                        kstart = q0c // 128
                        n_t = g.KT - kstart
                        ngr = (n_t + 1) // 2
                        for gi in range(ngr):
                            kts = [kstart + 2 * gi + j for j in range(2)
                                   if 2 * gi + j < n_t]
                            groups.append((kts, q0c, kstart, gi, n_t, gi == ngr - 1))
                    NG = len(groups)
                    G0 = Gbase[0]

                    def emit_sp(G, h=h, i=i, groups=groups, G0=G0, tqA=tqA, tqB=tqB):
                        kts, q0c, kstart, gi, n_t, last_c = groups[G - G0]
                        b = G % 2
                        wS = list(ldh[h]) + ([tqA] if q0c < 1024 else [tqA, tqB])
                        if G >= 2:
                            wS.append(("ss", exp_tok[G - 2]))
                        for j, kt in enumerate(kts):
                            sb.op("t", (lambda i=i, kt=kt, q0c=q0c, b=b, j=j: lambda t:
                                  t.matmul(sp[b][:, j * 512:(j + 1) * 512],
                                           kh[i][:, kt * 128:kt * 128 + 128],
                                           qh[i][:, q0c:q0c + 512], start=True, stop=True))(),
                                  waits=wS if j == 0 else (),
                                  inc="ts" if j == len(kts) - 1 else None)
                        tS = sb.c("ts")
                        wE = [("ts", tS)]
                        if G >= 3:
                            wE.append(("ts", con_tok[G - 3]))
                        for j, kt in enumerate(kts):
                            pb = (2 * G + j) % 6
                            sb.op("s", (lambda b=b, j=j, pb=pb, kt=kt: lambda s:
                                  s.activation(pt[pb][:, :512], sp[b][:, j * 512:(j + 1) * 512],
                                               AF.Exp, bias=bias68[:, kt:kt + 1],
                                               scale=g.SC))(),
                                  waits=wE if j == 0 else (),
                                  inc="ss" if j == len(kts) - 1 else None)
                        exp_tok[G] = sb.c("ss")
                        dep, dep_eng = exp_tok[G], "ss"
                        for j, kt in enumerate(kts):
                            d = kt - kstart
                            if d < 4:
                                pb = (2 * G + j) % 6
                                dv = sb.op("v", (lambda pb=pb, d=d: lambda v:
                                           v.tensor_tensor(pt[pb][:, :512], pt[pb][:, :512],
                                                           l0diag[:, d * 512:(d + 1) * 512],
                                                           ALU.mult))(),
                                           waits=[("ss", exp_tok[G])], inc="vs")
                                dep, dep_eng = dv, "vs"
                        msk_tok[G] = (dep_eng, dep)

                    def emit_cons(G, h=h, i=i, groups=groups, G0=G0):
                        kts, q0c, kstart, gi, n_t, last_c = groups[G - G0]
                        first_w = [msk_tok[G]]
                        if gi == 0:
                            first_w.append(("vs", evac_v[0]))
                        for j, kt in enumerate(kts):
                            pb = (2 * G + j) % 6
                            idx = 2 * gi + j
                            sb.op("t", (lambda pb=pb, first=(idx == 0),
                                        last=(idx == n_t - 1): lambda t:
                                  t.matmul(srow[:], ones_cb[:], pt[pb][:, :512],
                                           start=first, stop=last))(),
                                  waits=first_w if j == 0 else ())
                            sb.op("t", (lambda i=i, kt=kt, pb=pb, first=(idx == 0),
                                        last=(idx == n_t - 1): lambda t:
                                  t.matmul(op_[:, :512], vth[i][:, kt * 128:kt * 128 + 128],
                                           pt[pb][:, :512], start=first, stop=last))(),
                                  inc="ts" if j == len(kts) - 1 else None)
                        con_tok[G] = sb.c("ts")
                        if last_c:
                            sb.op("v", (lambda q0c=q0c, tk=con_tok[G]: lambda v: v.tensor_copy(
                                sums0[0:1, q0c:q0c + 512], srow[:]))(),
                                waits=[("ts", con_tok[G])], inc="vs")
                            evac_v[0] = sb.op("v", (lambda q0c=q0c: lambda v: v.tensor_copy(
                                oh[:, q0c:q0c + 512], op_[:, :512]))(), inc="vs")

                    for G in range(G0, G0 + NG):
                        emit_sp(G)
                        if G - G0 == 65:
                            qh_mark[h] = sb.c("ts")
                        if G - G0 >= LAG:
                            emit_cons(G - LAG)
                    for G in range(G0 + NG - LAG, G0 + NG):
                        emit_cons(G)
                    Gbase[0] = G0 + NG
                    head_ts_end[h] = sb.c("ts")
                    if h + 1 < g.H:
                        load_head(h + 1)
                    f0 = g.QTOK - 512
                    sb.op("v", (lambda f0=f0: lambda v: v.tensor_tensor(
                        rowsA[0:1, :512], sums0[0:1, f0:], islast[0:1, :], ALU.mult))(),
                        waits=[("vs", sb.c("vs"))], inc="vs")
                    sb.op("v", (lambda f0=f0: lambda v: v.tensor_tensor(
                        sums0[0:1, f0:], sums0[0:1, f0:], rowsA[0:1, :512],
                        ALU.subtract))(), inc="vs", strict=True)
                    sfix = sb.op("s", (lambda: lambda s: s.mul(
                        rowsA[0:1, :512], islast[0:1, :], float(g.S)))(),
                        waits=[("vs", sb.c("vs"))], inc="ss")
                    sb.op("v", (lambda f0=f0: lambda v: v.tensor_tensor(
                        sums0[0:1, f0:], sums0[0:1, f0:], rowsA[0:1, :512], ALU.add))(),
                        waits=[("ss", sfix)], inc="vs", strict=True)
                    sb.op("v", (lambda h=h: lambda v: v.tensor_scalar(
                        tmpO[:, :], islast[:, :], vsum[:, h:h + 1], None, ALU.mult))(), inc="vs",
                        strict=True)
                    sb.op("v", (lambda f0=f0: lambda v: v.tensor_tensor(
                        oh[:, f0:], oh[:, f0:], tmpO[:, :], ALU.add))(), inc="vs", strict=True)
                    for q0c, w_ in qchunks:
                        sb.op("v", (lambda q0c=q0c: lambda v: v.reciprocal(
                            rowsA[0:1, :512], sums0[0:1, q0c:q0c + 512]))(), inc="vs",
                            strict=True)
                        tB = mm_acc(sb, bcp[:], [(ones_rf[:], rowsA[0:1, :512])],
                                    waits=[("vs", sb.c("vs"))])
                        sb.op("v", (lambda h=h, q0c=q0c: lambda v: v.tensor_tensor(
                            stream[:, h * g.QTOK + q0c: h * g.QTOK + q0c + 512],
                            oh[:, q0c:q0c + 512], bcp[:], ALU.mult))(),
                            waits=[("ts", tB)], inc="vs", strict=True)

            if stop_after == 2 and "stream" in dbg:
                sb.fence()
                for dc in range(g.DC):
                    sb.dma("y", dbg["stream"][dc * 128:(dc + 1) * 128, :],
                           stream[:, dc * g.QTOK:(dc + 1) * g.QTOK], chan="dbg")

        sb.phase = "P3"
        if stop_after is None or stop_after >= 3:
            sb.fence()
            with contextlib.ExitStack() as P3:
                wo_sb = sbuf("wo_sb", [128, g.DC * g.D], BF16, P3)
                xres_s = sbuf("xres_s", [128, g.DC * 512], BF16, P3)
                xres = [xres_s, xres_s]
                zbuf3v = sbuf("zbuf3v3", [128, g.DC * 512], BF16, P3)
                sq3 = sbuf("sq3", [128, 512], BF16, P3)
                t13 = sbuf("t13", [128, 512], F32, P3)
                zp = [psum(f"zp{i}", [128, 512], F32, P3) for i in range(2)]
                pp3 = {"psz": psum("psz3", [1, 512], F32, P3),
                       "psq": psum("psq3", [1, 512], F32, P3),
                       "bcm": psum("bcm3", [128, 512], F32, P3),
                       "bcr": psum("bcr3", [128, 512], F32, P3),
                       "sq": sq3, "t1": t13, "zready": {}, "bc_free": 0}
                lwo = load_wtile("g", wo_sb, wo, 0, g.D, chan="wk")
                sup_vs_end = {}
                for si, (s0, swd) in enumerate(g.ttiles(g.QTOK)):
                    ss_guard = sb.c("ss")
                    ldx = sb.dma("g", xres[si % 2][:, :g.DC * swd].rearrange(
                        "p (dc x) -> p dc x", dc=g.DC), wslice3(xkv, s0, swd),
                        waits=[("vs", sup_vs_end.get(si - 1, 0))], chan="xs%d" % (si % 2))
                    emit_convs(1)
                    zready = {}
                    for oc in range(g.DC):
                        pairs = [(wo_sb[:, dc * g.D + oc * 128: dc * g.D + oc * 128 + 128],
                                  stream[:, dc * g.QTOK + s0: dc * g.QTOK + s0 + swd])
                                 for dc in range(g.DC)]
                        tl = mm_acc(sb, zp[oc % 2][:, :swd], pairs,
                                    waits=[lwo, ldx, ("vs", zready.get(oc - 2, 0))])
                        zready[oc] = sb.op("v", (lambda oc=oc, i=si % 2, swd=swd: lambda v:
                                           v.tensor_tensor(zbuf3v[:, oc * 512: oc * 512 + swd],
                                                           zp[oc % 2][:, :swd],
                                                           xres[i][:, oc * swd: oc * swd + swd],
                                                           ALU.add))(),
                                           waits=[("ts", tl), ("ss", ss_guard)], inc="vs")
                    pp3["zready"] = zready
                    sup_vs_end[si] = sb.c("vs")

                    def put_out(oc, t1_, bcr_, W, s0=s0):
                        sb.op("v", (lambda oc=oc, W=W, s0=s0, t1_=t1_, bcr_=bcr_: lambda v:
                               v.tensor_tensor(stream[:, oc * g.QTOK + s0: oc * g.QTOK + s0 + W],
                                               t1_[:, :W], bcr_[:, :W], ALU.mult))(), inc="vs",
                               strict=True)

                    emit_ln(sb, swd, lambda oc, swd=swd: zbuf3v[:, oc * 512: oc * 512 + swd],
                            g.DC, put_out, False, pp3, g.D)

            if stop_after == 3 and "stream" in dbg:
                sb.fence()
                for dc in range(g.DC):
                    sb.dma("y", dbg["stream"][dc * 128:(dc + 1) * 128, :],
                           stream[:, dc * g.QTOK:(dc + 1) * g.QTOK], chan="dbg")

        sb.phase = "P4"
        if stop_after is None or stop_after >= 4:
            for nm in ("g0T", "u0T", "d0T"):
                ensure_conv(nm)
            sb.fence()
            if "streamA" in dbg:
                for dc_ in range(g.DC):
                    sb.dma("y", dbg["streamA"][dc_ * 128:(dc_ + 1) * 128, :],
                           stream[:, dc_ * g.QTOK:(dc_ + 1) * g.QTOK], chan="dbg")
            if "g0c" in dbg:
                sb.dma("y", dbg["g0c"], cache["g0T"][:, :], waits=conv_done["g0T"], chan="dbg")
            if "u0c" in dbg:
                sb.dma("y", dbg["u0c"], cache["u0T"][:, :], waits=conv_done["u0T"], chan="dbg")
            with contextlib.ExitStack() as P4:
                hdn = sbuf("hdn", [128, g.IC0 * 512], BF16, P4)
                wgt = [sbuf(f"wgt{i}", [128, g.DC * 128], BF16, P4) for i in range(2)]
                wut = [sbuf(f"wut{i}", [128, g.DC * 128], BF16, P4) for i in range(2)]
                wdt = [sbuf(f"wdt{i}", [128, g.IC0 * 128], BF16, P4) for i in range(2)]
                sg_t = sbuf("sg_t", [128, 512], BF16, P4)
                su_t = sbuf("su_t", [128, 512], BF16, P4)
                zbuf4v = sbuf("zbuf4v4", [128, g.DC * 512], BF16, P4)
                ysc = sbuf("ysc", [128, 512], F32, P4)
                sq4 = sbuf("sq4", [128, 512], BF16, P4)
                t14 = sbuf("t14", [128, 512], F32, P4)
                gs_sb = sbuf("gs_sb", [128, g.IC0], F32, P4)
                us_sb = sbuf("us_sb", [128, g.IC0], F32, P4)
                ds_sb = sbuf("ds_sb", [128, g.DC], F32, P4)
                pg = psum("pg", [128, 512], F32, P4)
                pu = psum("pu", [128, 512], F32, P4)
                py = [psum(f"py{i}", [128, 512], F32, P4) for i in range(2)]
                pp4 = {"psz": psum("psz4", [1, 512], F32, P4),
                       "psq": psum("psq4", [1, 512], F32, P4),
                       "bcm": psum("bcm4", [128, 512], F32, P4),
                       "bcr": psum("bcr4", [128, 512], F32, P4),
                       "sq": sq4, "t1": t14, "zready": {}, "bc_free": 0}
                sb.dma("y", gs_sb[:], gsI, chan="cst")
                sb.dma("y", us_sb[:], usI, chan="cst")
                cst4 = sb.dma("y", ds_sb[:], dsI, chan="cst")
                gu_t = {}
                dn_t, dn_v = {}, {}
                prev_sv = prev_hv = 0
                for si, (s0, swd) in enumerate(g.ttiles(g.QTOK)):
                    ss_guard = sb.c("ss")
                    hdn_guard = sb.c("ts")
                    emit_convs(3)
                    for ic in range(g.IC0):
                        ib = ic % 2
                        gidx = si * g.IC0 + ic
                        lg = load_wtile("y", wgt[ib], cache["g0T"], ic * 128, 128,
                                        waits=[("ts", gu_t.get(gidx - 2, 0))] + conv_done["g0T"],
                                        chan="wg%d" % ib)
                        lu = load_wtile("y", wut[ib], cache["u0T"], ic * 128, 128,
                                        waits=conv_done["u0T"], chan="wu%d" % ib)
                        pairs_g = [(wgt[ib][:, dc * 128: dc * 128 + 128],
                                    stream[:, dc * g.QTOK + s0: dc * g.QTOK + s0 + swd])
                                   for dc in range(g.DC)]
                        pairs_u = [(wut[ib][:, dc * 128: dc * 128 + 128],
                                    stream[:, dc * g.QTOK + s0: dc * g.QTOK + s0 + swd])
                                   for dc in range(g.DC)]
                        tg = mm_acc(sb, pg[:, :swd], pairs_g,
                                    waits=[lg, lu, cst4, ("ss", prev_sv), ("vs", prev_hv)])
                        tu = mm_acc(sb, pu[:, :swd], pairs_u)
                        gu_t[gidx] = tu
                        if si == 0 and ic == 0 and "pg0" in dbg:
                            dv = sb.op("v", (lambda swd=swd: lambda v: v.tensor_copy(
                                sq4[:, :swd], pg[:, :swd]))(), waits=[("ts", tg)], inc="vs")
                            sb.dma("y", dbg["pg0"][:, :swd], sq4[:, :swd], waits=[("vs", dv)], chan="dbg")
                        sv = sb.op("s", (lambda ic=ic, swd=swd: lambda s: s.activation(
                            sg_t[:, :swd], pg[:, :swd], AF.Silu, scale=gs_sb[:, ic:ic + 1]))(),
                            waits=[("ts", tg)], inc="ss")
                        prev_sv = sv
                        sb.op("v", (lambda ic=ic, swd=swd: lambda v: v.tensor_scalar(
                            su_t[:, :swd], pu[:, :swd], us_sb[:, ic:ic + 1], None, ALU.mult))(),
                            waits=[("ts", tu)], inc="vs")
                        if si == 0 and ic == 0 and "sg0" in dbg:
                            sb.dma("y", dbg["sg0"][:, :swd], sg_t[:, :swd], waits=[("ss", sv)], chan="dbg")
                            sb.dma("y", dbg["su0"][:, :swd], su_t[:, :swd], waits=[("vs", sb.c("vs"))], chan="dbg")
                        prev_hv = sb.op("v", (lambda ic=ic, swd=swd: lambda v:
                                        v.tensor_tensor(hdn[:, ic * 512: ic * 512 + swd],
                                                        sg_t[:, :swd], su_t[:, :swd],
                                                        ALU.mult))(),
                                        waits=[("ss", sv), ("ts", hdn_guard)], inc="vs")
                    hready = sb.c("vs")
                    if si == 0 and "hdn" in dbg:
                        sb.dma("y", dbg["hdn"][:, :], hdn[:, :], waits=[("vs", hready)], chan="dbg")
                    zready = {}
                    for oc in range(g.DC):
                        didx = si * g.DC + oc
                        ld = load_wtile("y", wdt[oc % 2], cache["d0T"], oc * 128, 128,
                                        waits=[("ts", dn_t.get(didx - 2, 0))] + conv_done["d0T"],
                                        chan="wd%d" % (oc % 2))
                        pairs = [(wdt[oc % 2][:, ic * 128: ic * 128 + 128],
                                  hdn[:, ic * 512: ic * 512 + swd]) for ic in range(g.IC0)]
                        tl = mm_acc(sb, py[oc % 2][:, :swd], pairs,
                                    waits=[ld, ("vs", hready), ("vs", dn_v.get(oc - 2, 0))])
                        dn_t[didx] = tl
                        sb.op("v", (lambda oc=oc, swd=swd: lambda v: v.tensor_scalar(
                            ysc[:, :swd], py[oc % 2][:, :swd], ds_sb[:, oc:oc + 1], None,
                            ALU.mult))(), waits=[("ts", tl)], inc="vs")
                        zv = sb.op("v", (lambda oc=oc, swd=swd, s0=s0: lambda v: v.tensor_tensor(
                            zbuf4v[:, oc * 512: oc * 512 + swd], ysc[:, :swd],
                            stream[:, oc * g.QTOK + s0: oc * g.QTOK + s0 + swd], ALU.add))(),
                            waits=[("ss", ss_guard)], inc="vs")
                        zready[oc] = zv
                        dn_v[oc] = zv
                    pp4["zready"] = zready

                    def put_out(oc, t1_, bcr_, W, s0=s0):
                        sb.op("v", (lambda oc=oc, W=W, s0=s0, t1_=t1_, bcr_=bcr_: lambda v:
                               v.tensor_tensor(stream[:, oc * g.QTOK + s0: oc * g.QTOK + s0 + W],
                                               t1_[:, :W], bcr_[:, :W], ALU.mult))(), inc="vs",
                               strict=True)

                    emit_ln(sb, swd, lambda oc, swd=swd: zbuf4v[:, oc * 512: oc * 512 + swd],
                            g.DC, put_out, False, pp4, g.D)

            if stop_after == 4 and "stream" in dbg:
                sb.fence()
                for dc in range(g.DC):
                    sb.dma("y", dbg["stream"][dc * 128:(dc + 1) * 128, :],
                           stream[:, dc * g.QTOK:(dc + 1) * g.QTOK])

        sb.phase = "P5"
        if stop_after is None or stop_after >= 5:
            sb.fence()
            with contextlib.ExitStack() as P5:
                Asb2 = [sbuf(f"Asb{i}", [128, g.QTOK], BF16, P5) for i in range(2)]
                Bsb2 = [sbuf(f"Bsb{i}", [128, g.QTOK], BF16, P5) for i in range(2)]
                sc1 = sbuf("sc1", [128, g.QTOK], F32, P5)
                sc2 = sbuf("sc2", [128, g.QTOK], F32, P5)
                sc3 = sbuf("sc3", [128, g.QTOK], F32, P5)
                sc4 = sbuf("sc4", [128, g.QTOK], F32, P5)
                cosb = sbuf("cosb", [128, g.QTOK], F32, P5)
                sinb = sbuf("sinb", [128, g.QTOK], F32, P5)
                wt5 = [sbuf(f"wt5{i}", [128, g.DC * 128], BF16, P5) for i in range(2)]
                vst5 = sbuf("vst5", [128, 512], BF16, P5)
                vtt5 = sbuf("vtt5", [128, 512], BF16, P5)
                pq5 = [psum(f"pq5{i}", [128, 512], F32, P5) for i in range(2)]
                ptr5 = psum("ptr5", [128, 512], BF16, P5)

                sb.dma("y", cosb[:], ropecI, chan="cst")
                tbl = sb.dma("y", sinb[:], ropesI, chan="cst")
                bank_ev = {0: 0, 1: 0}
                wt_ts = {}

                def proj(ob, dst_sb, a0, alen, extra=()):
                    lw = load_wtile("g", wt5[ob % 2], wqkvT, ob * 128, 128,
                                    waits=[("ts", wt_ts.get(ob - 2, 0))], chan="wt%d" % (ob % 2))
                    evl = 0
                    for (t0, tw) in g.ttiles(alen):
                        b = (t0 // 512) % 2
                        pairs = [(wt5[ob % 2][:, dc * 128: dc * 128 + 128],
                                  stream[:, dc * g.QTOK + a0 + t0: dc * g.QTOK + a0 + t0 + tw])
                                 for dc in range(g.DC)]
                        tl = mm_acc(sb, pq5[b][:, :tw], pairs,
                                    waits=[lw, ("vs", bank_ev[b])] + list(extra))
                        evl = sb.op("v", (lambda b=b, t0=t0, tw=tw, dst_sb=dst_sb: lambda v:
                                    v.tensor_copy(dst_sb[:, t0:t0 + tw], pq5[b][:, :tw]))(),
                                    waits=[("ts", tl)], inc="vs")
                        bank_ev[b] = evl
                    wt_ts[ob] = sb.c("ts")
                    return evl

                sb.phase = "P5a"
                spills_par = {0: [], 1: []}
                spill_i = [0]
                pi = 0
                for grp, base in enumerate((0, g.DC)):
                    isq = base == 0
                    a0 = g.HALO if isq else 0
                    alen = g.CH if isq else g.QTOK
                    dstd = qrd if isq else krd
                    for p in range(g.H // 2):
                        emit_convs(2)
                        Asb = Asb2[pi % 2]
                        Bsb = Bsb2[pi % 2]
                        guard = list(spills_par[pi % 2])
                        obA, obB = base + 2 * p, base + 2 * p + 1
                        evA = proj(obA, Asb, a0, alen, extra=guard)
                        evB = proj(obB, Bsb, a0, alen, extra=guard)
                        cs = cosb[:, a0:a0 + alen]
                        sn = sinb[:, a0:a0 + alen]
                        w0 = [("vs", max(evA, evB)), tbl]
                        sb.op("v", (lambda cs=cs, L=alen, Asb=Asb: lambda v: v.tensor_tensor(
                            sc1[:, :L], Asb[:, :L], cs, ALU.mult))(), waits=w0, inc="vs")
                        sb.op("v", (lambda sn=sn, L=alen, Asb=Asb: lambda v: v.tensor_tensor(
                            sc2[:, :L], Asb[:, :L], sn, ALU.mult))(), inc="vs")
                        sb.op("v", (lambda cs=cs, L=alen, Bsb=Bsb: lambda v: v.tensor_tensor(
                            sc3[:, :L], Bsb[:, :L], cs, ALU.mult))(), inc="vs")
                        sb.op("v", (lambda sn=sn, L=alen, Bsb=Bsb: lambda v: v.tensor_tensor(
                            sc4[:, :L], Bsb[:, :L], sn, ALU.mult))(), inc="vs")
                        sb.op("v", (lambda L=alen, Asb=Asb: lambda v: v.tensor_tensor(
                            Asb[:, :L], sc1[:, :L], sc4[:, :L], ALU.subtract))(), inc="vs",
                            strict=True)
                        rp = sb.op("v", (lambda L=alen, Bsb=Bsb: lambda v: v.tensor_tensor(
                            Bsb[:, :L], sc3[:, :L], sc2[:, :L], ALU.add))(), inc="vs",
                            strict=True)
                        h0, h1 = 2 * p, 2 * p + 1
                        toks = []
                        for hh, src_ in ((h0, Asb[0:64, :alen]), (h1, Asb[64:128, :alen]),
                                         (h0 + g.H, Bsb[0:64, :alen]), (h1 + g.H, Bsb[64:128, :alen])):
                            ro = (hh % g.H) * 128 + (64 if hh >= g.H else 0)
                            ch = "sp%d" % (spill_i[0] % 4)
                            spill_i[0] += 1
                            toks.append(sb.dma("y", dstd[ro: ro + 64, :], src_,
                                               waits=[("vs", rp)], chan=ch))
                        spills_par[pi % 2] = toks
                        pi += 1
                sb.phase = "P5b"
                v_dma = ("ts", 0)
                tr5_ts = 0
                for vb in range(g.DC):
                    emit_convs(2)
                    ob = 2 * g.DC + vb
                    lw = load_wtile("g", wt5[ob % 2], wqkvT, ob * 128, 128,
                                    waits=[("ts", wt_ts.get(ob - 2, 0))], chan="wt%d" % (ob % 2))
                    for (t0, tw) in g.ttiles(g.QTOK):
                        b = (t0 // 512) % 2
                        pairs = [(wt5[ob % 2][:, dc * 128: dc * 128 + 128],
                                  stream[:, dc * g.QTOK + t0: dc * g.QTOK + t0 + tw])
                                 for dc in range(g.DC)]
                        tl = mm_acc(sb, pq5[b][:, :tw], pairs,
                                    waits=[lw, ("vs", bank_ev[b])])
                        ev = sb.op("v", (lambda b=b, tw=tw: lambda v: v.tensor_copy(
                            vst5[:, :tw], pq5[b][:, :tw]))(),
                            waits=[("ts", tl), ("ts", tr5_ts)], inc="vs")
                        bank_ev[b] = ev
                        for z in range(tw // 128):
                            sb.op("t", (lambda z=z: lambda t: t.transpose(
                                ptr5[:, z * 128:(z + 1) * 128],
                                vst5[:, z * 128:(z + 1) * 128], identb[:]))(),
                                waits=[("vs", ev)] if z == 0 else (),
                                inc="ts" if z == tw // 128 - 1 else None)
                        tr5_ts = sb.c("ts")
                        vz = sb.op("v", (lambda tw=tw: lambda v: v.tensor_copy(
                            vtt5[:, :tw], ptr5[:, :tw]))(),
                            waits=[("ts", tr5_ts), v_dma], inc="vs")
                        v_dma = sb.dma(
                            "y",
                            v1td[vb * 128:(vb + 1) * 128, t0:t0 + tw],
                            vtt5[:, :tw],
                            waits=[("vs", vz)], chan="vt5")
                    wt_ts[ob] = sb.c("ts")

            if stop_after == 5:
                sb.fence()
                if "qr" in dbg:
                    sb.dma("y", dbg["qr"], qrd[:, :], chan="dbg")
                if "kr" in dbg:
                    sb.dma("y", dbg["kr"], krd[:, :], chan="dbg")
                if "v1t" in dbg:
                    sb.dma("y", dbg["v1t"], v1td[:, :], chan="dbg")

        sb.phase = "P6"
        if stop_after is None or stop_after >= 6:
            sb.fence()
            P67 = top.enter_context(contextlib.ExitStack())
            attn1 = sbuf("attn1", [128, g.DC * g.CH], BF16, P67)
            with contextlib.ExitStack() as P6:
                krh = sbuf("krh", [128, g.QTOK], BF16, P6)
                v1th = sbuf("v1th", [128, g.QT * 128], BF16, P6)
                qrh = sbuf("qrh", [128, g.CH], BF16, P6)
                pt6 = [sbuf(f"pt6{i}", [128, 512], BF16, P6) for i in range(6)]
                oh6 = sbuf("oh6", [128, g.CH], BF16, P6)
                sums6 = sbuf("sums6", [1, g.CH], F32, P6)

                l1m = sbuf("l1m_s", [128, (8 + g.HALO // 128) * 512], BF16, P6)
                sp6 = [psum(f"sp6{i}", [128, 1024], F32, P6) for i in range(2)]
                op6 = psum("op6", [128, 512], F32, P6)
                srow6 = psum("srow6", [1, 512], F32, P6)
                bcp6 = psum("bcp6", [128, 512], F32, P6)

                cst6 = sb.dma("y", l1m[:], l1mI, chan="cst")
                LAG6 = 2
                exp6_tok, con6_tok, msk6_tok = {}, {}, {}
                G6base = [0]
                evac6_v = [0]
                h6_end = {}
                h6_mark = {}
                for h in range(g.H):
                    emit_convs(2)
                    kb6, vb6, qb6 = krh, v1th, qrh
                    rows = slice(h * 128, (h + 1) * 128)
                    wA = [("ts", h6_mark.get(h - 1, 0))]
                    wB = [("ts", h6_end.get(h - 1, 0))]
                    tkA1 = sb.dma("y", kb6[:, 0:1024], krd[rows, 0:1024], waits=wA, chan="l1k")
                    tkA2 = sb.dma("y", vb6[:, 0:1024], v1td[rows, 0:1024], waits=wA, chan="l1v")
                    tkA3 = sb.dma("y", qb6[:, 0:1024], qrd[rows, 0:1024], waits=wA, chan="l1q")
                    tkB1 = sb.dma("y", kb6[:, 1024:g.QTOK], krd[rows, 1024:g.QTOK],
                                  waits=wB, chan="l1k")
                    tkB2 = sb.dma("y", vb6[:, 1024:g.QT * 128], v1td[rows, 1024:g.QT * 128],
                                  waits=wB, chan="l1v")
                    tkB3 = sb.dma("y", qb6[:, 1024:g.CH], qrd[rows, 1024:g.CH],
                                  waits=wB, chan="l1q")
                    ldqA = [tkA1, tkA2, tkA3, cst6]
                    ldqB = [tkB1, tkB2, tkB3]
                    groups = []
                    for ci in range(g.CH // 512):
                        qa = g.HALO + 512 * ci
                        lo_t = max(0, (qa - g.W + 1) // 128)
                        hi_t = (qa + 511) // 128
                        n_t = hi_t - lo_t + 1
                        ngr = (n_t + 1) // 2
                        for gi in range(ngr):
                            kts = [lo_t + 2 * gi + j for j in range(2) if 2 * gi + j < n_t]
                            groups.append((kts, ci, lo_t, gi, n_t, gi == ngr - 1))
                    NG = len(groups)
                    G0 = G6base[0]

                    def emit_sp6(G, groups=groups, G0=G0, ldqA=ldqA, ldqB=ldqB,
                                 kb6=kb6, qb6=qb6):
                        kts, ci, lo_t, gi, n_t, last_c = groups[G - G0]
                        b = G % 2
                        wS = list(ldqA) if ci == 0 else list(ldqA + ldqB)
                        if G >= 2:
                            wS.append(("ss", exp6_tok[G - 2]))
                        for j, kt in enumerate(kts):
                            sb.op("t", (lambda kt=kt, ci=ci, b=b, j=j, kb6=kb6, qb6=qb6: lambda t:
                                  t.matmul(sp6[b][:, j * 512:(j + 1) * 512],
                                           kb6[:, kt * 128:kt * 128 + 128],
                                           qb6[:, 512 * ci:512 * ci + 512],
                                           start=True, stop=True))(),
                                  waits=wS if j == 0 else (),
                                  inc="ts" if j == len(kts) - 1 else None)
                        tS = sb.c("ts")
                        wE = [("ts", tS)]
                        if G >= 3:
                            wE.append(("ts", con6_tok[G - 3]))
                        for j, kt in enumerate(kts):
                            pb = (2 * G + j) % 6
                            sb.op("s", (lambda b=b, j=j, pb=pb: lambda s: s.activation(
                                pt6[pb][:, :512], sp6[b][:, j * 512:(j + 1) * 512],
                                AF.Exp, scale=g.SC))(),
                                waits=wE if j == 0 else (),
                                inc="ss" if j == len(kts) - 1 else None)
                        exp6_tok[G] = sb.c("ss")
                        for j, kt in enumerate(kts):
                            pb = (2 * G + j) % 6
                            delta = kt - lo_t
                            mi = delta if kt >= g.HALO // 128 else 8 + kt
                            sb.op("v", (lambda pb=pb, mi=mi: lambda v: v.tensor_tensor(
                                pt6[pb][:, :512], pt6[pb][:, :512],
                                l1m[:, mi * 512:(mi + 1) * 512], ALU.mult))(),
                                waits=[("ss", exp6_tok[G])] if j == 0 else (),
                                inc="vs" if j == len(kts) - 1 else None)
                        msk6_tok[G] = sb.c("vs")

                    def emit_cons6(G, groups=groups, G0=G0, vb6=vb6):
                        kts, ci, lo_t, gi, n_t, last_c = groups[G - G0]
                        first_w = [("vs", msk6_tok[G])]
                        if gi == 0:
                            first_w.append(("vs", evac6_v[0]))
                        for j, kt in enumerate(kts):
                            pb = (2 * G + j) % 6
                            idx = 2 * gi + j
                            sb.op("t", (lambda pb=pb, first=(idx == 0),
                                        last=(idx == n_t - 1): lambda t:
                                  t.matmul(srow6[:], ones_cb[:], pt6[pb][:, :512],
                                           start=first, stop=last))(),
                                  waits=first_w if j == 0 else ())
                            sb.op("t", (lambda kt=kt, pb=pb, first=(idx == 0),
                                        last=(idx == n_t - 1), vb6=vb6: lambda t: t.matmul(
                                op6[:, :512], vb6[:, kt * 128:kt * 128 + 128],
                                pt6[pb][:, :512], start=first, stop=last))(),
                                inc="ts" if j == len(kts) - 1 else None)
                        con6_tok[G] = sb.c("ts")
                        if last_c:
                            sb.op("v", (lambda ci=ci, tk=con6_tok[G]: lambda v: v.tensor_copy(
                                sums6[0:1, 512 * ci:512 * ci + 512], srow6[:]))(),
                                waits=[("ts", con6_tok[G])], inc="vs")
                            evac6_v[0] = sb.op("v", (lambda ci=ci: lambda v: v.tensor_copy(
                                oh6[:, 512 * ci:512 * ci + 512], op6[:, :512]))(),
                                inc="vs")

                    for G in range(G0, G0 + NG):
                        emit_sp6(G)
                        if G - G0 >= LAG6:
                            emit_cons6(G - LAG6)
                            if (G - LAG6) - G0 == 7:
                                h6_mark[h] = sb.c("ts")
                    for G in range(G0 + NG - LAG6, G0 + NG):
                        emit_cons6(G)
                    G6base[0] = G0 + NG
                    for ci in range(g.CH // 512):
                        sb.op("v", (lambda ci=ci: lambda v: v.reciprocal(
                            rowsA[0:1, :512], sums6[0:1, 512 * ci:512 * ci + 512]))(),
                            waits=[("vs", sb.c("vs"))], inc="vs")
                        tB = mm_acc(sb, bcp6[:], [(ones_rf[:], rowsA[0:1, :512])],
                                    waits=[("vs", sb.c("vs"))])
                        sb.op("v", (lambda h=h, ci=ci: lambda v: v.tensor_tensor(
                            attn1[:, h * g.CH + 512 * ci: h * g.CH + 512 * ci + 512],
                            oh6[:, 512 * ci:512 * ci + 512], bcp6[:], ALU.mult))(),
                            waits=[("ts", tB)], inc="vs", strict=True)
                    h6_end[h] = sb.c("ts")

            if stop_after == 6 and "attn1" in dbg:
                sb.fence()
                for dc in range(g.DC):
                    sb.dma("y", dbg["attn1"][dc * 128:(dc + 1) * 128, :],
                           attn1[:, dc * g.CH:(dc + 1) * g.CH], chan="dbg")

        sb.phase = "P7"
        if stop_after is None or stop_after >= 7:
            ensure_conv("woutT")
            sb.fence()
            with contextlib.ExitStack() as P7:
                wot = [sbuf(f"wot{i}", [128, g.DC * 128], BF16, P7) for i in range(2)]
                zbuf7v = sbuf("zbuf7v7", [128, g.DC * 512], BF16, P7)
                sq7 = sbuf("sq7", [128, 512], BF16, P7)
                t17 = sbuf("t17", [128, 512], F32, P7)
                zp7 = [psum(f"zp7{i}", [128, 512], F32, P7) for i in range(2)]
                pp7 = {"psz": psum("psz7", [1, 512], F32, P7),
                       "psq": psum("psq7", [1, 512], F32, P7),
                       "bcm": psum("bcm7", [128, 512], F32, P7),
                       "bcr": psum("bcr7", [128, 512], F32, P7),
                       "sq": sq7, "t1": t17, "zready": {}, "bc_free": 0}
                wo_ts = {}
                x3_dma = {}
                for si, (s0, swd) in enumerate(g.ttiles(g.CH)):
                    ss_guard = sb.c("ss")
                    zready = {}
                    for oc in range(g.DC):
                        widx = si * g.DC + oc
                        lw = load_wtile("y", wot[oc % 2], cache["woutT"], oc * 128, 128,
                                        waits=[("ts", wo_ts.get(widx - 2, 0))] + conv_done["woutT"],
                                        chan="wg%d" % (oc % 2))
                        pairs = [(wot[oc % 2][:, dc * 128: dc * 128 + 128],
                                  attn1[:, dc * g.CH + s0: dc * g.CH + s0 + swd])
                                 for dc in range(g.DC)]
                        tl = mm_acc(sb, zp7[oc % 2][:, :swd], pairs,
                                    waits=[lw, ("vs", zready.get(oc - 2, 0))])
                        wo_ts[widx] = tl
                        zready[oc] = sb.op("v", (lambda oc=oc, swd=swd, s0=s0: lambda v:
                                           v.tensor_tensor(zbuf7v[:, oc * 512: oc * 512 + swd],
                                                           zp7[oc % 2][:, :swd],
                                                           stream[:, oc * g.QTOK + g.HALO + s0:
                                                                  oc * g.QTOK + g.HALO + s0 + swd],
                                                           ALU.add))(),
                                           waits=[("ts", tl), ("ss", ss_guard)], inc="vs")
                    pp7["zready"] = zready

                    def put_out(oc, t1_, bcr_, W, s0=s0, si=si):
                        wv = sb.op("v", (lambda oc=oc, W=W, t1_=t1_, bcr_=bcr_: lambda v:
                                   v.tensor_tensor(zbuf7v[:, oc * 512: oc * 512 + W],
                                                   t1_[:, :W], bcr_[:, :W], ALU.mult))(),
                                   waits=[x3_dma.get((si - 1, oc), ("ts", 0))], inc="vs",
                                   strict=True)
                        x3_dma[(si, oc)] = sb.dma(
                            "y", x3d[oc * 128:(oc + 1) * 128, s0:s0 + W],
                            zbuf7v[:, oc * 512: oc * 512 + W], waits=[("vs", wv)],
                            chan="x3%d" % (oc % 2))

                    emit_ln(sb, swd, lambda oc, swd=swd: zbuf7v[:, oc * 512: oc * 512 + swd],
                            g.DC, put_out, False, pp7, g.D)
                    pp7["bc_free"] = sb.c("vs")

            if stop_after == 7 and "x3" in dbg:
                sb.fence()
                sb.dma("y", dbg["x3"], x3d[:, :], chan="dbg")

        sb.phase = "P8"
        if stop_after is None or stop_after >= 8:
            if stop_after is None or stop_after >= 6:
                P67.close()
            P27.close()
            for nm in ("w1g", "w1u", "w1d"):
                ensure_conv(nm)
            sb.fence()
            with contextlib.ExitStack() as P8:
                hdn1 = sbuf("hdn1", [128, g.IC1 * 512], BF16, P8)
                x3sl2 = [sbuf(f"x3sl{i}", [128, g.DC * 512], BF16, P8) for i in range(2)]
                wg1 = [sbuf(f"wg1{i}", [128, g.DC * 128], BF16, P8) for i in range(2)]
                wu1 = [sbuf(f"wu1{i}", [128, g.DC * 128], BF16, P8) for i in range(2)]
                wd1 = [sbuf(f"wd1{i}", [128, g.IC1 * 128], BF16, P8) for i in range(2)]
                sg1 = sbuf("sg1", [128, 512], BF16, P8)
                zf = sbuf("zf", [128, g.DC * 512], F32, P8)
                am = [sbuf(f"am{i}", [128, 512], BF16, P8) for i in range(4)]
                mx1 = sbuf("mx1", [128, 512], BF16, P8)
                mx2 = sbuf("mx2", [128, 512], BF16, P8)
                m2t = sbuf("m2t", [128, 512], BF16, P8)
                outT = sbuf("outT", [128, g.D // 2], F32, P8)
                sq8 = sbuf("sq8", [128, 512], F32, P8)
                t18 = sq8
                pg8 = psum("pg8", [128, 512], F32, P8)
                pu8 = psum("pu8", [128, 512], F32, P8)
                py8_ = psum("py8", [128, 512], F32, P8)
                py8 = [py8_, py8_]
                pp8 = {"psz": psum("psz8", [1, 512], F32, P8),
                       "psq": psum("psq8", [1, 512], F32, P8),
                       "bcm": psum("bcm8", [128, 512], F32, P8),
                       "bcr": psum("bcr8", [128, 512], F32, P8),
                       "sq": sq8, "t1": t18, "zready": {}, "bc_free": 0}
                ptr8 = psum("ptr8", [128, 512], F32, P8)

                gu1t = {}
                wd_ts8 = {}
                x_ts = {}
                zdone = {}
                pending = {}
                ydma_half = {}
                ydma_last = [("ts", 0)]
                prev_sv1 = prev_hv1 = 0
                sup8 = g.ttiles(g.CH)

                def ldx8(si):
                    s0_, swd_ = sup8[si]
                    return sb.dma(
                        "y", x3sl2[si % 2][:, :g.DC * swd_].rearrange(
                            "p (dc x) -> p dc x", dc=g.DC),
                        x3d.rearrange("(dc p) x -> p dc x", p=128)[:, :, s0_:s0_ + swd_],
                        waits=[("ts", x_ts.get(si - 2, 0)), ("vs", zdone.get(si - 2, 0))],
                        chan="x8")

                pending[0] = ldx8(0)
                for si, (s0, swd) in enumerate(sup8):
                    ss_guard = sb.c("ss")
                    hdn_guard = sb.c("ts")
                    if si + 1 < len(sup8):
                        pending[si + 1] = ldx8(si + 1)
                    ldx = pending[si]
                    x3sl = x3sl2[si % 2]
                    def sparsify(gb, swd=swd):
                        tiles = [hdn1[:, (m * g.GB1 + gb) * 512:(m * g.GB1 + gb) * 512 + swd]
                                 for m in range(4)]
                        gbv = sb.c("vs")
                        for m in range(4):
                            sb.op("s", (lambda m=m, tiles=tiles, swd=swd: lambda s: s.activation(
                                am[m][:, :swd], tiles[m], AF.Abs))(),
                                waits=[("vs", gbv)], inc="ss")
                        sw = sb.c("ss")
                        net = ((mx1, am[0], am[1], ALU.max, False),
                               (mx2, am[2], am[3], ALU.max, False),
                               (mx1, mx1, mx2, ALU.min, True),
                               (mx2, am[0], am[1], ALU.min, False),
                               (m2t, am[2], am[3], ALU.min, False),
                               (mx2, mx2, m2t, ALU.max, True),
                               (m2t, mx1, mx2, ALU.max, True))
                        for gi8, (o, a, b2, opk, st) in enumerate(net):
                            sb.op("v", (lambda o=o, a=a, b2=b2, opk=opk, swd=swd: lambda v:
                                   v.tensor_tensor(o[:, :swd], a[:, :swd], b2[:, :swd], opk))(),
                                   waits=[("ss", sw)] if gi8 == 0 else (), inc="vs", strict=st)
                        for m in range(4):
                            sb.op("v", (lambda m=m, swd=swd: lambda v: v.tensor_tensor(
                                am[m][:, :swd], am[m][:, :swd], m2t[:, :swd], ALU.is_ge))(),
                                inc="vs", strict=(m == 0))
                        for m in range(4):
                            sb.op("v", (lambda m=m, tiles=tiles, swd=swd: lambda v: v.tensor_tensor(
                                tiles[m], tiles[m], am[m][:, :swd], ALU.mult))(),
                                inc="vs", strict=(m == 0))

                    for ic in range(g.IC1):
                        ib = ic % 2
                        gidx = si * g.IC1 + ic
                        lg = load_wtile("y", wg1[ib], cache["w1g"], ic * 128, 128,
                                        waits=[("ts", gu1t.get(gidx - 2, 0))] + conv_done["w1g"],
                                        chan="wg%d" % ib)
                        lu = load_wtile("y", wu1[ib], cache["w1u"], ic * 128, 128,
                                        waits=conv_done["w1u"], chan="wu%d" % ib)
                        pairs_g = [(wg1[ib][:, dc * 128: dc * 128 + 128],
                                    x3sl[:, dc * swd: dc * swd + swd]) for dc in range(g.DC)]
                        pairs_u = [(wu1[ib][:, dc * 128: dc * 128 + 128],
                                    x3sl[:, dc * swd: dc * swd + swd]) for dc in range(g.DC)]
                        tg = mm_acc(sb, pg8[:, :swd], pairs_g,
                                    waits=[lg, lu, ldx,
                                           ("ss", prev_sv1), ("vs", prev_hv1)])
                        tu = mm_acc(sb, pu8[:, :swd], pairs_u)
                        gu1t[gidx] = tu
                        sv = sb.op("s", (lambda swd=swd: lambda s: s.activation(
                            sg1[:, :swd], pg8[:, :swd], AF.Silu))(), waits=[("ts", tg)], inc="ss")
                        prev_sv1 = sv
                        prev_hv1 = sb.op("v", (lambda ic=ic, swd=swd: lambda v:
                                         v.tensor_tensor(hdn1[:, ic * 512: ic * 512 + swd],
                                                         sg1[:, :swd], pu8[:, :swd],
                                                         ALU.mult))(),
                                         waits=[("ss", sv), ("ts", tu), ("ts", hdn_guard)],
                                         inc="vs")
                        if ic >= 3 * g.GB1:
                            sparsify(ic - 3 * g.GB1)
                    hready = sb.c("vs")
                    zready = {}
                    for oc in range(g.DC):
                        didx8 = si * g.DC + oc
                        ld = load_wtile("y", wd1[oc % 2], cache["w1d"], oc * 128, 128,
                                        waits=[("ts", wd_ts8.get(didx8 - 2, 0))] + conv_done["w1d"],
                                        chan="wd%d" % (oc % 2))
                        pairs = [(wd1[oc % 2][:, ic * 128: ic * 128 + 128],
                                  hdn1[:, ic * 512: ic * 512 + swd]) for ic in range(g.IC1)]
                        tl = mm_acc(sb, py8[oc % 2][:, :swd], pairs,
                                    waits=[ld, ("vs", hready), ("vs", zready.get(oc - 1, 0))])
                        wd_ts8[didx8] = tl
                        zready[oc] = sb.op("v", (lambda oc=oc, swd=swd, x3sl=x3sl: lambda v:
                            v.tensor_tensor(
                            zf[:, oc * 512: oc * 512 + swd], py8[oc % 2][:, :swd],
                            x3sl[:, oc * swd: oc * swd + swd], ALU.add))(),
                            waits=[("ts", tl), ("ss", ss_guard)], inc="vs")
                    pp8["zready"] = zready
                    zdone[si] = sb.c("vs")
                    x_ts[si] = sb.c("ts")

                    def put_out(oc, t1_, bcr_, W):
                        sb.op("v", (lambda oc=oc, W=W, bcr_=bcr_: lambda v:
                               v.tensor_tensor(zf[:, oc * 512: oc * 512 + W],
                                               zf[:, oc * 512: oc * 512 + W],
                                               bcr_[:, :W], ALU.mult))(), inc="vs",
                               strict=True)

                    emit_ln(sb, swd, lambda oc, swd=swd: zf[:, oc * 512: oc * 512 + swd],
                            g.DC, put_out, True, pp8, g.D)
                    lnv = sb.c("vs")
                    trv = 0
                    gsz = min(4, g.DC // 2)
                    for tb in range(swd // 128):
                        for hf in range(2):
                            for oc4 in range(hf * g.DC // 2, (hf + 1) * g.DC // 2, gsz):
                                for k4 in range(gsz):
                                    oc = oc4 + k4
                                    sb.op("t", (lambda oc=oc, tb=tb, k4=k4: lambda t: t.transpose(
                                        ptr8[:, k4 * 128:(k4 + 1) * 128],
                                        zf[:, oc * 512 + tb * 128: oc * 512 + tb * 128 + 128],
                                        identf[:]))(),
                                        waits=[("vs", lnv), ydma_half.get(hf, ("ts", 0)),
                                               ("vs", trv)] if k4 == 0 else (),
                                        inc="ts" if k4 == gsz - 1 else None)
                                tt4 = sb.c("ts")
                                trv = sb.op("v", (lambda oc4=oc4, hf=hf, gsz=gsz: lambda v:
                                    v.tensor_copy(
                                    outT[:, (oc4 - hf * g.DC // 2) * 128:
                                            (oc4 - hf * g.DC // 2 + gsz) * 128], ptr8[:, :gsz * 128]))(),
                                    waits=[("ts", tt4), ydma_last[0]], inc="vs")
                            ydma_last[0] = ydma_half[hf] = sb.dma(
                                "y", y[s0 + tb * 128: s0 + tb * 128 + 128,
                                       hf * g.D // 2:(hf + 1) * g.D // 2], outT[:, :g.D // 2],
                                waits=[("vs", trv)], chan="yo%d" % hf)
                    x_ts[si] = sb.c("ts")

        sb.fence()
        for k_, v_ in sb.cnt.items():
            assert v_ < 60000, ("semaphore overflow risk", k_, v_)
        nins = sum(len(q) for q in sb.q.values())
        sb.flush(blk)
        build_program.last_stats = {"cnt": dict(sb.cnt), "n_ops": nins,
                                    "n_chans": len(sb.chans), "marks": sb.marks}

    return nc


def kernel(**inputs):
    g = G_FULL
    sh, cores = stage_host(g, inputs)
    nc = build_program(g)
    in_maps = [dict(sh, **cores[c]) for c in range(8)]
    res = run_bass_kernel_spmd(nc, in_maps, core_ids=list(range(8)))
    out = np.empty((g.B, g.S, g.D), dtype=np.float32)
    for c in range(8):
        b, j = c // 4, c % 4
        out[b, j * g.CH:(j + 1) * g.CH, :] = res.results[c]["y"]
    return out
